# version 16
# baseline (speedup 1.0000x reference)
import numpy as np
from contextlib import ExitStack

import concourse.bass as bass
import concourse.mybir as mybir
from concourse.bass_utils import run_bass_kernel_spmd

F32 = mybir.dt.float32
BF16 = mybir.dt.bfloat16
AF = mybir.ActivationFunctionType
ALU = mybir.AluOpType
AXX = mybir.AxisListType.X

MAGIC = 12582912.0
K2SCALE = float(np.float32(0.02)) / 32.0
EPS_LN = 1e-5
EPS_KC = 1e-6
EPS_Q = 1e-5
D = 1024
INNER = 4096
DC = D // 128
DEBUG_ATTN = False


class Sem:

    def __init__(self, st, nc, name):
        self.h = st.enter_context(nc.semaphore(name))
        self.n = 0

    def inc(self, instr, k=1):
        instr.then_inc(self.h, k)
        self.n += k
        return instr

    def wait(self, eng, n=None):
        eng.wait_ge(self.h, self.n if n is None else n)


def build_nc(L=4096, T=512):
    NCH = L // T
    TT = T // 128
    assert T % 128 == 0 and L % T == 0 and TT == 4

    nc = bass.Bass("TRN2", target_bir_lowering=False)
    x_hbm = nc.dram_tensor("x", [L, D], F32, kind="ExternalInput")
    y_hbm = nc.dram_tensor("y", [L, D], F32, kind="ExternalOutput")
    wq1t = nc.dram_tensor("wq1t", [D, D], F32, kind="ExternalInput")
    wk1t = nc.dram_tensor("wk1t", [D, D], F32, kind="ExternalInput")
    wk2t = nc.dram_tensor("wk2t", [D, D], F32, kind="ExternalInput")
    wpt = nc.dram_tensor("wpt", [D, D], F32, kind="ExternalInput")
    wgt = nc.dram_tensor("wgt", [D, INNER], F32, kind="ExternalInput")
    wut = nc.dram_tensor("wut", [D, INNER], F32, kind="ExternalInput")
    wdt = nc.dram_tensor("wdt", [INNER, D], F32, kind="ExternalInput")
    bq1 = nc.dram_tensor("bq1", [D], F32, kind="ExternalInput")
    bk1 = nc.dram_tensor("bk1", [D], F32, kind="ExternalInput")
    bk2 = nc.dram_tensor("bk2", [D], F32, kind="ExternalInput")
    bpr = nc.dram_tensor("bpr", [D], F32, kind="ExternalInput")
    sc_g = nc.dram_tensor("sc_g", [8, 128, 8, 512], BF16, kind="Internal")
    sc_u = nc.dram_tensor("sc_u", [8, 128, 8, 512], BF16, kind="Internal")
    sc_d = nc.dram_tensor("sc_d", [8, 128, 4, 1024], BF16, kind="Internal")
    if DEBUG_ATTN:
        dbg = nc.dram_tensor("dbg", [6, 128, 4104], F32, kind="ExternalOutput")
        dbgq = nc.dram_tensor("dbgq", [2, 128, 4104], BF16, kind="ExternalOutput")
        dbg2 = nc.dram_tensor("dbg2", [128, 4104], F32, kind="ExternalOutput")

    st = ExitStack()
    E = st.enter_context
    xsb = [E(nc.sbuf_tensor(f"xsb{i}", [128, TT, D], F32)) for i in range(2)]
    pool = E(nc.sbuf_tensor("pool", [128, 25616], F32))
    r3 = lambda ap, a: ap.rearrange("p (a b) -> p a b", a=a)
    VT = r3(pool[:, 0:4096], 8)
    KB = r3(pool[:, 4096:8192], 8)
    QB = r3(pool[:, 8192:12296], 8)
    KV = r3(pool[:, 12296:16392], 8)
    OAR = r3(pool[:, 16392:20496], 8)
    K2B = r3(pool[:, 20496:24592], 8)
    silu2 = pool[:, 24592:25616]
    hbuf = r3(pool[:, 0:TT * 4096], TT)
    qstage = pool[:, 0:TT * 1024]
    tpiece = pool[:, 20496:24592]
    poolb = E(nc.sbuf_tensor("poolb", [128, 20488], BF16))
    aq = poolb[:, 0:4096].rearrange("p (a b c) -> p a b c", a=TT, b=DC)
    hq = poolb[:, 4096:20480].rearrange("p (a b c) -> p a b c", a=TT, b=32)
    xn2 = r3(poolb[:, 0:4096], TT)
    xn2T = r3(poolb[:, 4096:8200], 8)
    OB_ = r3(poolb[:, 8200:12296], 8)
    tstage = poolb[:, 12296:16392]
    awb = poolb[:, 12296:20488]
    sarena = E(nc.sbuf_tensor("sarena", [128, 12288], BF16))
    gsl = sarena[:, 0:4096]
    usl = sarena[:, 4096:8192]
    dsl = sarena[:, 8192:12288]
    ident = E(nc.sbuf_tensor("ident", [128, 128], F32))
    iota_p = pool[:, 0:128]
    iota_f = pool[:, 128:256]
    onesc = E(nc.sbuf_tensor("onesc", [128, 1], F32))
    onesr32 = E(nc.sbuf_tensor("onesr32", [1, 128], F32))
    onesrb = E(nc.sbuf_tensor("onesrb", [1, 128], BF16))
    bprow = E(nc.sbuf_tensor("bprow", [1, D], BF16))
    bsb = E(nc.sbuf_tensor("bsb", [128, 3 * DC], F32))
    smal = E(nc.sbuf_tensor("smal", [128, 288], F32))
    mean1 = E(nc.sbuf_tensor("mean1", [1, 8], F32))
    stats = smal[:, 0:48]
    mv = smal[:, 48:56]
    rstd = smal[:, 56:60]
    nmr = smal[:, 60:64]
    hi = smal[:, 64:68]
    lo = smal[:, 68:72]
    t1 = smal[:, 72:76]
    t2 = smal[:, 76:80]
    cmax = smal[:, 80:84]
    s1 = smal[:, 84:88]
    Aq = smal[:, 88:92]
    Bq = smal[:, 92:96]
    cg = smal[:, 96:100]
    cu = smal[:, 100:104]
    hmax = smal[:, 104:108]
    s2v = smal[:, 108:112]
    cdv = smal[:, 112:116]
    kc_carry = smal[:, 116:124]
    kv_carry = smal[:, 124:132]
    s2_carry = smal[:, 132:140]
    swb = smal[:, 140:143]
    asum = smal[:, 144:160]
    oarc = smal[:, 160:168]
    qc = smal[:, 168:176]
    xnc = smal[:, 176:184]
    preS = [smal[:, 192:224], smal[:, 224:256]]
    magic_ap = smal[:, 184:185]
    cmb = smal[:, 188:191]
    epsln_ap = smal[:, 185:186]
    epskc_ap = smal[:, 186:187]
    nmagic_ap = smal[:, 187:188]
    PB = E(nc.psum_tensor("PB", [128, 4096], F32))
    pb = [PB[:, k * 512:(k + 1) * 512] for k in range(8)]

    dX = [Sem(st, nc, "dX0"), Sem(st, nc, "dX1")]
    dO = Sem(st, nc, "dO")
    dGU = [Sem(st, nc, "dGU0"), Sem(st, nc, "dGU1")]
    dDN = [Sem(st, nc, "dDN0"), Sem(st, nc, "dDN1")]
    dTI = [Sem(st, nc, f"dTI{i}") for i in range(4)]
    dTG = [Sem(st, nc, f"dTG{i}") for i in range(4)]
    dTU = [Sem(st, nc, f"dTU{i}") for i in range(4)]
    dTD = [Sem(st, nc, f"dTD{i}") for i in range(4)]
    dAW = [Sem(st, nc, "dAW0"), Sem(st, nc, "dAW1"),
           Sem(st, nc, "dAW2"), Sem(st, nc, "dAW3")]
    dBP = Sem(st, nc, "dBP")
    dB = Sem(st, nc, "dB")
    dDB = Sem(st, nc, "dDB")
    cV = Sem(st, nc, "cV")
    cS = Sem(st, nc, "cS")
    cT = Sem(st, nc, "cT")
    cG = Sem(st, nc, "cG")

    V, S, TE, SY, G = nc.vector, nc.scalar, nc.tensor, nc.sync, nc.gpsimd

    G.iota(iota_p[:, :], [[0, 128]], channel_multiplier=1,
           allow_small_or_imprecise_dtypes=True)
    cG.inc(G.iota(iota_f[:, :], [[1, 128]], channel_multiplier=0,
                  allow_small_or_imprecise_dtypes=True))
    cG.wait(V)
    V.tensor_tensor(out=ident[:], in0=iota_p[:], in1=iota_f[:], op=ALU.is_equal)
    V.memset(onesc[:], 1.0)
    V.memset(onesr32[:], 1.0)
    V.memset(onesrb[:], 1.0)
    V.memset(kc_carry, 0.0)
    V.memset(kv_carry, 0.0)
    V.memset(s2_carry, 0.0)
    V.memset(oarc, 0.0)
    V.memset(qc, 0.0)
    V.memset(xnc, 0.0)
    V.memset(magic_ap, MAGIC)
    V.memset(epsln_ap, EPS_LN)
    V.memset(epskc_ap, EPS_KC)
    cV.inc(V.memset(nmagic_ap, -MAGIC))
    dB.inc(SY.dma_start(out=bsb[:, 0:DC],
                        in_=bq1.ap().rearrange("(j p) -> p j", p=128),
                        allow_slow_non_contiguous=True), 16)
    dB.inc(SY.dma_start(out=bsb[:, DC:2 * DC],
                        in_=bk1.ap().rearrange("(j p) -> p j", p=128),
                        allow_slow_non_contiguous=True), 16)
    dB.inc(SY.dma_start(out=bsb[:, 2 * DC:3 * DC],
                        in_=bk2.ap().rearrange("(j p) -> p j", p=128),
                        allow_slow_non_contiguous=True), 16)
    dBP.inc(G.dma_start(out=bprow[:, :], in_=bpr.ap()[None, :]), 16)
    b_q1 = bsb[:, 0:DC]
    b_k1 = bsb[:, DC:2 * DC]
    b_k2s = bsb[:, 2 * DC:3 * DC]
    dB.wait(V)
    cV.inc(V.tensor_scalar(out=b_k2s, in0=b_k2s, scalar1=K2SCALE, scalar2=None,
                           op0=ALU.mult))
    dX[0].inc(SY.dma_start(out=xsb[0][:, :, :],
                           in_=x_hbm.ap()[0:T, :]
                           .rearrange("(a p) d -> p a d", p=128)), 16)
    tern_done = {}

    ttr = smal[:, 191:192]
    tpc = [tpiece[:, 0:2048], tpiece[:, 2048:4096],
           pool[:, 16392:18440], pool[:, 18440:20488]]
    tst = [poolb[:, 12296:14344], poolb[:, 14344:16392],
           poolb[:, 8200:10248], poolb[:, 10248:12296]]
    tw_state = {"idx": 0, "slotV": [None] * 4, "stgV": [None] * 4}

    def tern_dma_in(src_ap, pf):
        i = tw_state["idx"]
        tw_state["idx"] += 1
        bsel = i % 4
        buf = tpc[bsel][:, 0:pf]
        if tw_state["slotV"][bsel] is not None:
            sem_, n_ = tw_state["slotV"][bsel]
            SY.wait_ge(sem_.h, n_)
        else:
            cV.wait(SY)
        dTI[bsel].inc(SY.dma_start(out=buf, in_=src_ap), 16)
        return bsel, buf

    def mean_piece(src_ap, pf, acc_slot):
        bsel, buf = tern_dma_in(src_ap, pf)
        dTI[bsel].wait(S)
        cS.inc(S.activation(out=buf, in_=buf, func=AF.Abs,
                            accum_out=acc_slot))
        tw_state["slotV"][bsel] = (cS, cS.n)

    def mean_finalize(sw_slot, inv_n, cm_slot, cmf):
        cS.wait(V)
        cV.inc(V.reduce_sum(out=ttr, in_=asum[:, :], axis=AXX))
        cV.wait(TE)
        cS.wait(TE)
        cT.inc(TE.matmul(out=pb[0][0:1, 0:1], lhsT=onesc[:, :], rhs=ttr,
                         start=True, stop=True))
        cT.wait(V)
        cV.inc(V.tensor_scalar(out=mean1[:, 0:1], in0=pb[0][0:1, 0:1],
                               scalar1=inv_n, scalar2=1e-5,
                               op0=ALU.mult, op1=ALU.max))
        cV.wait(V)
        V.reciprocal(out=mean1[:, 2:3], in_=mean1[:, 0:1])
        cV.inc(V.tensor_scalar(out=mean1[:, 3:4], in0=mean1[:, 0:1],
                               scalar1=cmf, scalar2=None, op0=ALU.mult))
        cV.wait(TE)
        cS.wait(TE)
        TE.matmul(out=pb[1][:, 0:1], lhsT=onesr32[0:1, :],
                  rhs=mean1[0:1, 2:3], start=True, stop=True)
        cT.inc(TE.matmul(out=pb[2][:, 0:1], lhsT=onesr32[0:1, :],
                         rhs=mean1[0:1, 3:4], start=True, stop=True))
        cT.wait(V)
        V.tensor_copy(out=sw_slot, in_=pb[1][:, 0:1])
        cV.inc(V.tensor_copy(out=cm_slot, in_=pb[2][:, 0:1]))

    def tern_mean(w_hbm, n_rows, n_free, sw_slot, inv_n, cm_slot, cmf):
        kchunks = n_rows // 128
        cV.wait(S)
        idx = 0
        if n_free >= 2048:
            for kc in range(kchunks):
                for j in range(n_free // 2048):
                    src = bass.AP(w_hbm, kc * 128 * n_free + j * 2048,
                                  [[n_free, 128], [1, 2048]])
                    mean_piece(src, 2048, asum[:, idx:idx + 1])
                    idx += 1
        else:
            pr = 2048 // n_free
            for kc in range(0, kchunks, pr):
                src = bass.AP(w_hbm, kc * 128 * n_free,
                              [[n_free, 128], [128 * n_free, pr], [1, n_free]])
                mean_piece(src, n_free * pr, asum[:, idx:idx + 1])
                idx += 1
        mean_finalize(sw_slot, inv_n, cm_slot, cmf)

    def tern_piece(src_ap, dst_ap, sw_slot, pf, osems, sw_wait=False):
        bsel, buf = tern_dma_in(src_ap, pf)
        sbf = tst[bsel][:, 0:pf]
        dTI[bsel].wait(V)
        cV.wait(V)
        cV.inc(V.tensor_scalar(out=buf, in0=buf, scalar1=sw_slot, scalar2=MAGIC,
                               op0=ALU.mult, op1=ALU.add))
        cV.wait(V)
        cV.inc(V.tensor_scalar(out=buf, in0=buf, scalar1=MAGIC, scalar2=1.0,
                               op0=ALU.subtract, op1=ALU.min))
        cV.wait(V)
        prev = tw_state["stgV"][bsel]
        if prev is not None:
            prev[0].wait(V, prev[1])
        cV.inc(V.tensor_scalar(out=sbf, in0=buf, scalar1=-1.0, scalar2=None,
                               op0=ALU.max))
        tw_state["slotV"][bsel] = (cV, cV.n)
        cV.wait(G)
        osem = osems[bsel]
        osem.inc(G.dma_start(out=dst_ap, in_=sbf), 16)
        tw_state["stgV"][bsel] = (osem, osem.n)

    def tern_write_gu(w_hbm, scratch, sw_slot, osems):
        for ob in range(8):
            for icg in range(2):
                src = bass.AP(w_hbm, icg * 4 * 128 * INNER + ob * 512,
                              [[INNER, 128], [128 * INNER, 4], [1, 512]])
                dst = bass.AP(scratch, ob * (128 * 8 * 512) + icg * 4 * 512,
                              [[8 * 512, 128], [512, 4], [1, 512]])
                tern_piece(src, dst, sw_slot, 2048, osems,
                           sw_wait=(ob == 0 and icg == 0))

    def tern_write_d(w_hbm, scratch, sw_slot, osems):
        for icg in range(8):
            for h2 in range(2):
                src = bass.AP(w_hbm, (icg * 4 + h2 * 2) * 128 * D,
                              [[D, 128], [128 * D, 2], [1, D]])
                dst = bass.AP(scratch, icg * (128 * 4 * 1024) + h2 * 2 * 1024,
                              [[4 * 1024, 128], [1024, 2], [1, 1024]])
                tern_piece(src, dst, sw_slot, 2048, osems,
                           sw_wait=(icg == 0 and h2 == 0))

    def emit_tern_stream():
        tern_mean(wgt, D, INNER, swb[:, 0:1], 1.0 / (D * INNER), cmb[:, 0:1],
                  1.0 / 127.0)
        tern_write_gu(wgt, sc_g, swb[:, 0:1], dTG)
        tern_mean(wut, D, INNER, swb[:, 1:2], 1.0 / (D * INNER), cmb[:, 1:2],
                  1.0 / 127.0)
        tern_write_gu(wut, sc_u, swb[:, 1:2], dTU)
        tern_mean(wdt, INNER, D, swb[:, 2:3], 1.0 / (D * INNER), cmb[:, 2:3],
                  1.0 / 254.0)
        tern_write_d(wdt, sc_d, swb[:, 2:3], dTD)

    def ln_stats_part(xb, tts):
        cV.wait(V)
        last = None
        for tt in tts:
            V.bn_stats(out=stats[:, tt * 12:tt * 12 + 6], in_=xb[:, tt, 0:512])
            V.bn_stats(out=stats[:, tt * 12 + 6:tt * 12 + 12],
                       in_=xb[:, tt, 512:1024])
            V.reduce_max(out=hi[:, tt:tt + 1], in_=xb[:, tt, :], axis=AXX)
            last = V.tensor_reduce(out=lo[:, tt:tt + 1], in_=xb[:, tt, :],
                                   axis=AXX, op=ALU.min)
        cV.inc(last)

    def ln_stats(xb, skip01=False):
        ln_stats_part(xb, (2, 3) if skip01 else (0, 1, 2, 3))
        cV.wait(V)
        for tt in range(TT - 1):
            V.bn_aggr(out=mv[:, tt * 2:tt * 2 + 2],
                      in_=stats[:, tt * 12:tt * 12 + 12])
        tt = TT - 1
        cV.inc(V.bn_aggr(out=mv[:, tt * 2:tt * 2 + 2],
                         in_=stats[:, tt * 12:tt * 12 + 12]))

    def ln_finalize():
        cV.wait(S)
        cS.inc(S.activation(out=rstd, in_=mv[:, 1:8:2], func=AF.Ln, bias=epsln_ap))
        cS.wait(S)
        cS.inc(S.activation(out=rstd, in_=rstd, func=AF.Exp, scale=-0.5))

    def quant_scales():
        cS.wait(V)
        V.tensor_tensor(out=t1, in0=hi, in1=mv[:, 0:8:2], op=ALU.subtract)
        cV.inc(V.tensor_tensor(out=t2, in0=mv[:, 0:8:2], in1=lo, op=ALU.subtract))
        cV.wait(V)
        cV.inc(V.tensor_tensor(out=cmax, in0=t1, in1=t2, op=ALU.max))
        cV.wait(V)
        cV.inc(V.tensor_tensor(out=t1, in0=rstd, in1=cmax, op=ALU.mult))
        cV.wait(V)
        cV.inc(V.tensor_scalar(out=t1, in0=t1, scalar1=EPS_Q, scalar2=None,
                               op0=ALU.max))
        cV.wait(V)
        cV.inc(V.reciprocal(out=t2, in_=t1))
        cV.wait(V)
        cV.inc(V.scalar_tensor_tensor(out=Aq, in0=t2, scalar=127.0, in1=rstd,
                                      op0=ALU.mult, op1=ALU.mult))
        cV.wait(V)
        cV.inc(V.tensor_tensor(out=Bq, in0=mv[:, 0:8:2], in1=Aq, op=ALU.mult))
        cV.wait(V)
        cV.inc(V.tensor_scalar(out=Bq, in0=Bq, scalar1=-1.0, scalar2=MAGIC,
                               op0=ALU.mult, op1=ALU.add))

    def transpose_quads(quads, evac="copy", eng="alt"):
        savedV = [cV.n] * 4
        savedS = [cS.n] * 4
        for qi, (srcs, dst) in enumerate(quads):
            b = qi % 4
            TE.wait_ge(cV.h, savedV[b])
            TE.wait_ge(cS.h, savedS[b])
            for j, s_ in enumerate(srcs[:-1]):
                TE.transpose(out=PB[:, b * 1024 + j * 128:
                                    b * 1024 + (j + 1) * 128],
                             in_=s_, identity=ident[:, :])
            cT.inc(TE.transpose(
                out=PB[:, b * 1024 + (len(srcs) - 1) * 128:
                       b * 1024 + len(srcs) * 128],
                in_=srcs[-1], identity=ident[:, :]))
            pt = PB[:, b * 1024:b * 1024 + len(srcs) * 128]
            if eng == "V" or (eng == "alt" and qi % 2 == 0):
                cT.wait(V)
                if evac == "copy":
                    cV.inc(V.tensor_copy(out=dst, in_=pt))
                else:
                    cV.inc(V.tensor_scalar(out=dst, in0=pt, scalar1=MAGIC,
                                           scalar2=None, op0=ALU.subtract))
                savedV[b] = cV.n
            else:
                cT.wait(S)
                if evac == "copy":
                    cS.inc(S.copy(out=dst, in_=pt))
                else:
                    cS.inc(S.add(out=dst, in_=pt, add=nmagic_ap))
                savedS[b] = cS.n

    def xstage(tt):
        hs, ti = tt // 2, tt % 2
        return VT[:, :, hs * 256 + ti * 128:hs * 256 + ti * 128 + 128]

    def attn_pre_tts(xb, tts):
        ln_stats_part(xb, tts)
        cV.wait(V)
        last = None
        for tt in tts:
            last = V.bn_aggr(out=mv[:, tt * 2:tt * 2 + 2],
                             in_=stats[:, tt * 12:tt * 12 + 12])
        cV.inc(last)
        t0 = tts[0]
        cV.wait(S)
        cS.inc(S.activation(out=rstd[:, t0:t0 + 2],
                            in_=mv[:, 2 * t0 + 1:2 * t0 + 4:2],
                            func=AF.Ln, bias=epsln_ap))
        cS.wait(S)
        cS.inc(S.activation(out=rstd[:, t0:t0 + 2], in_=rstd[:, t0:t0 + 2],
                            func=AF.Exp, scale=-0.5))
        cS.wait(V)
        cV.inc(V.scalar_tensor_tensor(out=nmr[:, t0:t0 + 2],
                                      in0=mv[:, 2 * t0:2 * t0 + 4:2],
                                      scalar=-1.0, in1=rstd[:, t0:t0 + 2],
                                      op0=ALU.mult, op1=ALU.mult))
        cV.wait(S)
        sl_ = None
        for tt in tts:
            sl_ = S.activation(out=xstage(tt), in_=xb[:, tt, :],
                               func=AF.Identity, scale=rstd[:, tt:tt + 1],
                               bias=nmr[:, tt:tt + 1])
        cS.inc(sl_)

    def emit_ffn_pre(c, xb, first_of_chunk, p, skip01=False):
        ps = preS[p]
        pmv, prstd = ps[:, 0:8], ps[:, 8:12]
        pAq, pBq = ps[:, 12:16], ps[:, 16:20]
        pcg, pcu = ps[:, 20:24], ps[:, 24:28]
        pt1 = ps[:, 28:32]
        if first_of_chunk:
            dX[c % 2].wait(V)
        ln_stats(xb, skip01=skip01)
        cV.wait(S)
        cS.inc(S.activation(out=prstd, in_=mv[:, 1:8:2], func=AF.Ln,
                            bias=epsln_ap))
        cS.wait(S)
        cS.inc(S.activation(out=prstd, in_=prstd, func=AF.Exp, scale=-0.5))
        cS.wait(V)
        cV.inc(V.tensor_copy(out=pmv, in_=mv))
        cV.wait(V)
        V.tensor_tensor(out=t1, in0=hi, in1=mv[:, 0:8:2], op=ALU.subtract)
        cV.inc(V.tensor_tensor(out=t2, in0=mv[:, 0:8:2], in1=lo,
                               op=ALU.subtract))
        cV.wait(V)
        cV.inc(V.tensor_tensor(out=cmax, in0=t1, in1=t2, op=ALU.max))
        cV.wait(V)
        cV.inc(V.tensor_tensor(out=pt1, in0=prstd, in1=cmax, op=ALU.mult))
        cV.wait(V)
        cV.inc(V.tensor_scalar(out=pt1, in0=pt1, scalar1=EPS_Q, scalar2=None,
                               op0=ALU.max))
        cV.wait(V)
        cV.inc(V.reciprocal(out=t2, in_=pt1))
        cV.wait(V)
        cV.inc(V.scalar_tensor_tensor(out=pAq, in0=t2, scalar=127.0, in1=prstd,
                                      op0=ALU.mult, op1=ALU.mult))
        cV.wait(V)
        cV.inc(V.tensor_tensor(out=pBq, in0=pmv[:, 0:8:2], in1=pAq,
                               op=ALU.mult))
        cV.wait(V)
        cV.inc(V.tensor_scalar(out=pBq, in0=pBq, scalar1=-1.0, scalar2=MAGIC,
                               op0=ALU.mult, op1=ALU.add))
        cV.wait(S)
        sl_ = None
        for tt in range(TT):
            sl_ = S.activation(out=qstage[:, tt * 1024:(tt + 1) * 1024],
                               in_=xb[:, tt, :], func=AF.Identity,
                               scale=pAq[:, tt:tt + 1], bias=pBq[:, tt:tt + 1])
        cS.inc(sl_)
        cS.wait(TE)
        transpose_quads(
            [([qstage[:, tt * 1024 + ic * 128:tt * 1024 + (ic + 1) * 128]
               for ic in range(DC)],
              aq[:, tt, :, :]) for tt in range(TT)], evac="submagic")

    def emit_ffn_rest(c, xb, p, pre_next=None, attn_next=None):
        cg = preS[p][:, 20:24]
        cu = preS[p][:, 24:28]
        pt1 = preS[p][:, 28:32]
        cV.wait(V)
        V.tensor_scalar(out=cg, in0=pt1, scalar1=cmb[:, 0:1], scalar2=None,
                        op0=ALU.mult)
        cV.inc(V.tensor_scalar(out=cu, in0=pt1, scalar1=cmb[:, 1:2],
                               scalar2=None, op0=ALU.mult))
        gu_arena_cT = [cT.n, cT.n]
        evac_saved = [(cV.n, cS.n), (cV.n, cS.n)]
        slot_cV = [cV.n, cV.n]

        def gu_load(k):
            ob_, icg_ = divmod(k, 2)
            bs = k % 2
            S.wait_ge(cT.h, gu_arena_cT[bs])
            src_g = bass.AP(sc_g, ob_ * (128 * 8 * 512) + icg_ * 4 * 512,
                            [[8 * 512, 128], [512, 4], [1, 512]])
            src_u = bass.AP(sc_u, ob_ * (128 * 8 * 512) + icg_ * 4 * 512,
                            [[8 * 512, 128], [512, 4], [1, 512]])
            dGU[bs].inc(S.dma_start(
                out=gsl[:, bs * 2048:(bs + 1) * 2048], in_=src_g), 16)
            dGU[bs].inc(S.dma_start(
                out=usl[:, bs * 2048:(bs + 1) * 2048], in_=src_u), 16)

        if c == 0:
            for s_, n_ in tern_done["gu"]:
                s_.wait(S, n_)
        gu_load(0)
        gu_load(1)

        hpart = smal[:, 256:288]

        def gu_evac(ob_, ttg):
            cT.wait(S)
            cT.wait(V)
            for tt in (ttg * 2, ttg * 2 + 1):
                slot = tt % 2
                sl = silu2[:, slot * 512:slot * 512 + 512]
                S.wait_ge(cV.h, slot_cV[slot])
                cS.inc(S.activation(out=sl, in_=pb[ttg * 4 + (tt % 2)][:, :],
                                    func=AF.Silu, scale=cg[:, tt:tt + 1]))
                cS.wait(V)
                cV.inc(V.scalar_tensor_tensor(
                    out=hbuf[:, tt, ob_ * 512:(ob_ + 1) * 512],
                    in0=pb[ttg * 4 + 2 + (tt % 2)][:, :],
                    scalar=cu[:, tt:tt + 1], in1=sl,
                    op0=ALU.mult, op1=ALU.mult))
                cV.wait(V)
                cV.inc(V.reduce_max(out=hpart[:, tt * 8 + ob_:tt * 8 + ob_ + 1],
                                    in_=hbuf[:, tt, ob_ * 512:(ob_ + 1) * 512],
                                    axis=AXX, apply_absolute_value=True))
                slot_cV[slot] = cV.n
            evac_saved[ttg] = (cV.n, cS.n)

        first_mm = True
        for ob in range(8):
            for icg in range(2):
                k = ob * 2 + icg
                bsel = k % 2
                dGU[bsel].wait(TE)
                if first_mm:
                    cV.wait(TE)
                    cS.wait(TE)
                    first_mm = False
                for ttg in range(2):
                    if icg == 0:
                        TE.wait_ge(cV.h, evac_saved[ttg][0])
                        TE.wait_ge(cS.h, evac_saved[ttg][1])
                    tl_ = None
                    for ic4 in range(4):
                        ic = icg * 4 + ic4
                        for tt in (ttg * 2, ttg * 2 + 1):
                            TE.matmul(out=pb[ttg * 4 + (tt % 2)][:, :],
                                      lhsT=aq[:, tt, ic, :],
                                      rhs=gsl[:, bsel * 2048 + ic4 * 512:
                                              bsel * 2048 + (ic4 + 1) * 512],
                                      start=(ic == 0), stop=(ic == 7))
                            tl_ = TE.matmul(
                                out=pb[ttg * 4 + 2 + (tt % 2)][:, :],
                                lhsT=aq[:, tt, ic, :],
                                rhs=usl[:, bsel * 2048 + ic4 * 512:
                                        bsel * 2048 + (ic4 + 1) * 512],
                                start=(ic == 0), stop=(ic == 7))
                    cT.inc(tl_)
                    if icg == 1:
                        gu_evac(ob, ttg)
                gu_arena_cT[bsel] = cT.n
                if k + 2 < 16:
                    gu_load(k + 2)
        cV.wait(V)
        if c == 0:
            for s_, n_ in tern_done["d"]:
                s_.wait(V, n_)
                s_.wait(S, n_)
        for tt in range(TT):
            cV.inc(V.reduce_max(out=hmax[:, tt:tt + 1],
                                in_=hpart[:, tt * 8:tt * 8 + 8], axis=AXX))
            cV.wait(V)
            cV.inc(V.tensor_scalar(out=hmax[:, tt:tt + 1],
                                   in0=hmax[:, tt:tt + 1], scalar1=EPS_Q,
                                   scalar2=None, op0=ALU.max))
            cV.wait(V)
            V.reciprocal(out=t1[:, tt:tt + 1], in_=hmax[:, tt:tt + 1])
            cV.inc(V.tensor_scalar(out=cdv[:, tt:tt + 1],
                                   in0=hmax[:, tt:tt + 1],
                                   scalar1=cmb[:, 2:3], scalar2=None,
                                   op0=ALU.mult))
            cV.wait(V)
            cV.inc(V.tensor_scalar(out=s2v[:, tt:tt + 1], in0=t1[:, tt:tt + 1],
                                   scalar1=127.0, scalar2=None, op0=ALU.mult))
            cV.wait(S)
            cS.inc(S.activation(out=hbuf[:, tt, :], in_=hbuf[:, tt, :],
                                func=AF.Identity, scale=s2v[:, tt:tt + 1],
                                bias=magic_ap))
            cS.wait(TE)
            transpose_quads(
                [([hbuf[:, tt, ic * 128:(ic + 1) * 128]
                   for ic in range(q * 8, q * 8 + 8)],
                  hq[:, tt, q * 8:q * 8 + 8, :]) for q in range(4)],
                evac="submagic")
        if pre_next is not None:
            pre_next()
        dn_arena_cT = [cT.n, cT.n]

        def dn_load(ih_):
            bs = ih_ % 2
            S.wait_ge(cT.h, dn_arena_cT[bs])
            src_d = bass.AP(sc_d,
                            (ih_ // 2) * (128 * 4 * 1024) + (ih_ % 2) * 2048,
                            [[4 * 1024, 128], [1024, 2], [1, 1024]])
            dDN[bs].inc(S.dma_start(out=dsl[:, bs * 2048:(bs + 1) * 2048],
                                    in_=src_d), 16)

        if c == 0:
            for s_, n_ in tern_done["d"]:
                s_.wait(S, n_)
        dn_load(0)
        dn_load(1)
        for ih in range(16):
            bsel = ih % 2
            dDN[bsel].wait(TE)
            if ih == 0:
                cV.wait(TE)
                cS.wait(TE)
            tl_ = None
            for ic2 in range(2):
                ic = ih * 2 + ic2
                for tt in range(TT):
                    for obk in range(2):
                        tl_ = TE.matmul(
                            out=pb[tt * 2 + obk][:, :], lhsT=hq[:, tt, ic, :],
                            rhs=dsl[:, bsel * 2048 + ic2 * 1024 + obk * 512:
                                    bsel * 2048 + ic2 * 1024 + (obk + 1) * 512],
                            start=(ic == 0), stop=(ic == 31))
            cT.inc(tl_)
            dn_arena_cT[bsel] = cT.n
            if ih + 2 < 16:
                dn_load(ih + 2)
        cT.wait(V)
        vl_ = None
        for tt in (0, 1):
            for obk in range(2):
                vl_ = V.scalar_tensor_tensor(
                    out=xb[:, tt, obk * 512:(obk + 1) * 512],
                    in0=pb[tt * 2 + obk][:, :], scalar=cdv[:, tt:tt + 1],
                    in1=xb[:, tt, obk * 512:(obk + 1) * 512],
                    op0=ALU.mult, op1=ALU.add)
        cV.inc(vl_)
        if attn_next is not None:
            attn_next()
        cV.wait(V)
        vl_ = None
        for tt in (2, 3):
            for obk in range(2):
                vl_ = V.scalar_tensor_tensor(
                    out=xb[:, tt, obk * 512:(obk + 1) * 512],
                    in0=pb[tt * 2 + obk][:, :], scalar=cdv[:, tt:tt + 1],
                    in1=xb[:, tt, obk * 512:(obk + 1) * 512],
                    op0=ALU.mult, op1=ALU.add)
        cV.inc(vl_)

    kc_mid = smal[:, 144:152]
    kv_mid = smal[:, 152:160]
    s2_mid = smal[:, 40:48]

    def emit_attn(c, xb, pre01_done=False):
        HT = T // 2
        if not pre01_done:
            attn_pre_tts(xb, (0, 1))
        cS.wait(TE)
        cS.wait(V)
        aw_arena_cT = [cT.n, cT.n, cT.n, cT.n]
        proj_evac_cS = [cS.n, cS.n]

        def aw_load(w_hbm, k):
            bs = k % 4
            G.wait_ge(cT.h, aw_arena_cT[bs])
            src = bass.AP(w_hbm, (k % 4) * 2 * 128 * D,
                          [[D, 128], [128 * D, 2], [1, D]])
            dAW[bs].inc(G.dma_start(
                out=awb[:, bs * 2048:(bs + 1) * 2048], in_=src), 16)

        def aw_tile(k, et):
            bs = (k // 2) % 4
            off = bs * 2048 + (k % 2) * 1024 + et * 128
            return awb[:, off:off + 128]

        def proj(w_hbm, out_fm, out_off, func, bias_ap, scale, rhs_off, th):
            aw_load(w_hbm, 0)
            aw_load(w_hbm, 1)
            for dc in range(DC):
                kp = dc // 2
                bsel = kp % 4
                dAW[bsel].wait(TE)
                if dc == 0:
                    TE.wait_ge(cS.h, proj_evac_cS[0])
                    TE.wait_ge(cS.h, proj_evac_cS[1])
                tl_ = None
                for et in range(DC):
                    tl_ = TE.matmul(
                        out=pb[et][:, 0:HT],
                        lhsT=aw_tile(dc, et),
                        rhs=xn2T[:, dc, rhs_off + th * HT:
                                 rhs_off + th * HT + HT],
                        start=(dc == 0), stop=(dc == DC - 1))
                cT.inc(tl_)
                if dc % 2 == 1:
                    aw_arena_cT[bsel] = cT.n
                    if kp + 2 < 4:
                        aw_load(w_hbm, kp + 2)
            for eh in range(2):
                cT.wait(S)
                sl2 = None
                for ei in range(4):
                    et = eh * 4 + ei
                    sl2 = S.activation(
                        out=out_fm[:, et, out_off + th * HT:
                                   out_off + th * HT + HT],
                        in_=pb[eh * 4 + ei][:, 0:HT], func=func,
                        bias=bias_ap[:, et:et + 1], scale=scale)
                cS.inc(sl2)
                proj_evac_cS[eh] = cS.n

        def emit_half_mms(th):
            tt0 = th * 2
            transpose_quads(
                [([xstage(tt)[:, dc, :] for dc in range(DC)],
                  xn2T[:, :, 1 + tt * 128:1 + (tt + 1) * 128])
                 for tt in (tt0, tt0 + 1)])
            transpose_quads(
                [([xb[:, tt, dc * 128:(dc + 1) * 128] for dc in range(DC)],
                  VT[:, :, tt * 128:(tt + 1) * 128])
                 for tt in (tt0, tt0 + 1)])
            if th == 0:
                cV.wait(V)
                V.tensor_copy(out=xn2T[:, :, 0], in_=xnc)
                V.tensor_copy(out=QB[:, :, 0], in_=qc)
                cV.inc(V.tensor_copy(out=OAR[:, :, 0], in_=oarc))
            cV.wait(TE)
            cS.wait(TE)
            proj(wk1t, KB, 0, AF.Exp, b_k1, 1.0, 1, th)
            proj(wq1t, QB, 1, AF.Sigmoid, b_q1, 1.0, 1, th)
            proj(wk2t, K2B, 0, AF.Sigmoid, b_k2s, K2SCALE, 0, th)
            half_cS[th] = cS.n
            half_cV[th] = cV.n

        def chain(th):
            o0 = th * HT
            lane_cV = []
            for l in range(4):
                dl = slice(2 * l, 2 * l + 2)
                KBl = KB[:, dl, o0:o0 + HT]
                KVl = KV[:, dl, o0:o0 + HT]
                VTl = VT[:, dl, o0:o0 + HT]
                K2l = K2B[:, dl, o0:o0 + HT]
                if l == 0:
                    V.wait_ge(cS.h, half_cS[th])
                cV.wait(V)
                cV.inc(V.tensor_tensor(out=KVl, in0=KBl, in1=VTl, op=ALU.mult))
                cV.wait(V)
                for dc in (2 * l, 2 * l + 1):
                    ini = (kc_carry[:, dc:dc + 1] if th == 0
                           else kc_mid[:, dc:dc + 1])
                    ll = V.tensor_tensor_scan(out=KB[:, dc, o0:o0 + HT],
                                              data0=KB[:, dc, o0:o0 + HT],
                                              data1=KB[:, dc, o0:o0 + HT],
                                              initial=ini,
                                              op0=ALU.add, op1=ALU.bypass)
                cV.inc(ll)
                cV.wait(V)
                for dc in (2 * l, 2 * l + 1):
                    ini = (kv_carry[:, dc:dc + 1] if th == 0
                           else kv_mid[:, dc:dc + 1])
                    ll = V.tensor_tensor_scan(out=KV[:, dc, o0:o0 + HT],
                                              data0=KV[:, dc, o0:o0 + HT],
                                              data1=KV[:, dc, o0:o0 + HT],
                                              initial=ini,
                                              op0=ALU.add, op1=ALU.bypass)
                cV.inc(ll)
                cV.wait(V)
                mdst = (kc_mid, kv_mid) if th == 0 else (kc_carry, kv_carry)
                V.tensor_copy(out=mdst[0][:, dl], in_=KB[:, dl, o0 + HT - 1])
                cV.inc(V.tensor_copy(out=mdst[1][:, dl],
                                     in_=KV[:, dl, o0 + HT - 1]))
                cV.wait(S)
                cS.inc(S.activation(out=KBl, in_=KBl, func=AF.Ln,
                                    bias=epskc_ap))
                cS.wait(S)
                cS.inc(S.activation(out=KBl, in_=KBl, func=AF.Exp, scale=-1.0))
                cS.wait(V)
                cV.wait(V)
                cV.inc(V.tensor_tensor(out=KVl, in0=KVl, in1=KBl, op=ALU.mult))
                cV.wait(V)
                cV.inc(V.tensor_tensor(out=OAR[:, dl, o0 + 1:o0 + HT + 1],
                                       in0=KVl,
                                       in1=QB[:, dl, o0 + 1:o0 + HT + 1],
                                       op=ALU.mult))
                cV.wait(V)
                cV.inc(V.tensor_tensor(out=KBl, in0=VTl,
                                       in1=OAR[:, dl, o0:o0 + HT],
                                       op=ALU.subtract))
                if c == 0 and th == 0:
                    cV.wait(V)
                    cV.inc(V.memset(KB[:, dl, 0:1], 0.0))
                cV.wait(V)
                cV.inc(V.tensor_tensor(out=KVl, in0=K2l, in1=KBl,
                                       op=ALU.mult))
                cV.wait(V)
                for dc in (2 * l, 2 * l + 1):
                    ini = (s2_carry[:, dc:dc + 1] if th == 0
                           else s2_mid[:, dc:dc + 1])
                    ll = V.tensor_tensor_scan(out=KV[:, dc, o0:o0 + HT],
                                              data0=KV[:, dc, o0:o0 + HT],
                                              data1=KV[:, dc, o0:o0 + HT],
                                              initial=ini,
                                              op0=ALU.add, op1=ALU.bypass)
                cV.inc(ll)
                cV.wait(V)
                mdst = s2_mid if th == 0 else s2_carry
                cV.inc(V.tensor_copy(out=mdst[:, dl],
                                     in_=KV[:, dl, o0 + HT - 1]))
                cV.wait(V)
                cV.inc(V.scalar_tensor_tensor(out=VTl, in0=KVl,
                                              scalar=K2SCALE,
                                              in1=QB[:, dl, o0:o0 + HT],
                                              op0=ALU.mult, op1=ALU.mult))
                cV.wait(V)
                cV.inc(V.tensor_tensor(out=OB_[:, dl, o0:o0 + HT], in0=VTl,
                                       in1=OAR[:, dl, o0 + 1:o0 + HT + 1],
                                       op=ALU.add))
                lane_cV.append(cV.n)
            if th == 1:
                cV.wait(V)
                V.tensor_copy(out=oarc, in_=OAR[:, :, 512])
                V.tensor_copy(out=qc, in_=QB[:, :, 512])
                cV.inc(V.tensor_copy(out=xnc, in_=xn2T[:, :, 512]))
            return lane_cV

        def wproj_mms(th, lane_cV):
            TE.wait_ge(cV.h, wp_evac_cV[th])
            TE.wait_ge(cS.h, proj_evac_cS[th])
            aw_load(wpt, 0)
            aw_load(wpt, 1)
            TE.wait_ge(cV.h, lane_cV[0])
            for tt in (th * 2, th * 2 + 1):
                for eb in range(2):
                    TE.matmul(out=pb[th * 4 + (tt % 2) * 2 + eb][:, :],
                              lhsT=onesrb[:, :],
                              rhs=bprow[:, eb * 512:(eb + 1) * 512],
                              start=True, stop=False)
            for dc in range(DC):
                kp = dc // 2
                bsel = kp % 4
                dAW[bsel].wait(TE)
                if dc % 2 == 0:
                    TE.wait_ge(cV.h, lane_cV[kp])
                tl_ = None
                for tt in (th * 2, th * 2 + 1):
                    for eb in range(2):
                        tl_ = TE.matmul(
                            out=pb[th * 4 + (tt % 2) * 2 + eb][:, :],
                            lhsT=OB_[:, dc, tt * 128:(tt + 1) * 128],
                            rhs=aw_tile(dc, 0)[:, 0:0 + 512] if False else
                            awb[:, bsel * 2048 + (dc % 2) * 1024 + eb * 512:
                                bsel * 2048 + (dc % 2) * 1024 + (eb + 1) * 512],
                            start=False, stop=(dc == DC - 1))
                cT.inc(tl_)
                if dc % 2 == 1:
                    aw_arena_cT[bsel] = cT.n
                    if kp + 2 < 4:
                        aw_load(wpt, kp + 2)

        def wproj_evac(th):
            cT.wait(V)
            vl_ = None
            for tt in (th * 2, th * 2 + 1):
                for eb in range(2):
                    vl_ = V.tensor_tensor(
                        out=xb[:, tt, eb * 512:(eb + 1) * 512],
                        in0=pb[th * 4 + (tt % 2) * 2 + eb][:, :],
                        in1=xb[:, tt, eb * 512:(eb + 1) * 512],
                        op=ALU.add)
            cV.inc(vl_)
            wp_evac_cV[th] = cV.n

        wp_evac_cV = [cV.n, cV.n]
        half_cS = [cS.n, cS.n]
        half_cV = [cV.n, cV.n]
        if c == 0:
            for grp in (tern_done["gu"], tern_done["d"]):
                for s_, n_ in grp:
                    s_.wait(G, n_)
        emit_half_mms(0)
        attn_pre_tts(xb, (2, 3))
        emit_half_mms(1)
        if DEBUG_ATTN and c == 0:
            cS.wait(SY)
            cV.wait(SY)
            dDB.inc(SY.dma_start(out=bass.AP(dbg2, 0, [[4104, 128], [1, 4104]]),
                                 in_=pool[:, 8192:12296]), 16)
            dDB.wait(V)
            dDB.wait(S)
            dDB.wait(TE)
        if c == 0:
            for s_, n_ in tern_done["ti"] + tern_done["gu"] + tern_done["d"]:
                s_.wait(S, n_)
                s_.wait(V, n_)
        dBP.wait(TE, 16)
        lanes0 = chain(0)
        wproj_mms(0, lanes0)
        lanes1 = chain(1)
        if DEBUG_ATTN and c == 0:
            cV.wait(SY)
            cS.wait(SY)
            for i_, ap_ in enumerate([pool[:, 4096:8192], pool[:, 12296:16392],
                                      pool[:, 0:4096], pool[:, 16392:20496],
                                      pool[:, 8192:12296], pool[:, 20496:24592]]):
                n_ = ap_.shape[1]
                dDB.inc(SY.dma_start(out=bass.AP(dbg, i_ * 128 * 4104,
                                                [[4104, 128], [1, n_]]),
                                    in_=ap_), 16)
            dDB.inc(SY.dma_start(out=bass.AP(dbgq, 0, [[4104, 128], [1, 4104]]),
                                in_=poolb[:, 4096:8200]), 16)
            dDB.inc(SY.dma_start(out=bass.AP(dbgq, 128 * 4104,
                                            [[4104, 128], [1, 4096]]),
                                in_=poolb[:, 8200:12296]), 16)
            dDB.wait(V)
            dDB.wait(S)
            dDB.wait(TE)
        wproj_evac(0)
        ln_stats_part(xb, (0, 1))
        wproj_mms(1, lanes1)
        wproj_evac(1)

    emit_ffn_pre(0, xsb[0], True, 0)
    emit_tern_stream()
    tern_done["gu"] = [(x, x.n) for x in dTG + dTU]
    tern_done["d"] = [(x, x.n) for x in dTD]
    tern_done["ti"] = [(x, x.n) for x in dTI]

    for c in range(NCH):
        xb = xsb[c % 2]
        if c + 1 < NCH:
            if c >= 1:
                dO.wait(SY, 16 * c)
            dX[(c + 1) % 2].inc(SY.dma_start(
                out=xsb[(c + 1) % 2][:, :, :],
                in_=x_hbm.ap()[(c + 1) * T:(c + 2) * T, :]
                .rearrange("(a p) d -> p a d", p=128)), 16)
        emit_ffn_rest(c, xb, c % 2,
                      attn_next=(lambda: attn_pre_tts(xb, (0, 1))))
        emit_attn(c, xb, pre01_done=True)
        emit_ffn_pre(c, xb, False, c % 2, skip01=True)
        pn = None
        if c + 1 < NCH:
            pn = (lambda cc: (lambda: emit_ffn_pre(cc, xsb[cc % 2], True,
                                                   cc % 2)))(c + 1)
        emit_ffn_rest(c, xb, c % 2, pre_next=pn)
        cV.wait(SY)
        cS.wait(SY)
        dO.inc(SY.dma_start(
            out=y_hbm.ap()[c * T:(c + 1) * T, :].rearrange("(a p) d -> p a d",
                                                           p=128),
            in_=xb[:, :, :]), 16)
    SY.wait_ge(dO.h, 16 * NCH)
    st.close()
    return nc


def kernel(**inp):
    def f32c(a):
        return np.ascontiguousarray(np.asarray(a, dtype=np.float32))

    x = f32c(inp["L0_attn_x"])
    B, L, _ = x.shape
    wT = {
        "wq1t": f32c(np.asarray(inp["L0_attn_Wq1"]).T),
        "wk1t": f32c(np.asarray(inp["L0_attn_Wk1"]).T),
        "wk2t": f32c(np.asarray(inp["L0_attn_Wk2"]).T),
        "wpt": f32c(np.asarray(inp["L0_attn_Wproj"]).T),
        "wgt": f32c(np.asarray(inp["L0_ffn_w_gate"]).T),
        "wut": f32c(np.asarray(inp["L0_ffn_w_up"]).T),
        "wdt": f32c(np.asarray(inp["L0_ffn_w_down"]).T),
        "bq1": f32c(inp["L0_attn_bq1"]),
        "bk1": f32c(inp["L0_attn_bk1"]),
        "bk2": f32c(inp["L0_attn_bk2"]),
        "bpr": f32c(inp["L0_attn_bproj"]),
    }
    nc = build_nc(L=L, T=512)
    in_maps = [dict(x=x[c], **wT) for c in range(B)]
    res = run_bass_kernel_spmd(nc, in_maps, core_ids=list(range(B)))
    return np.stack([np.asarray(res.results[c]["y"], dtype=np.float32)
                     for c in range(B)], axis=0)


# version 17
# speedup vs baseline: 1.0287x; 1.0287x over previous
import numpy as np
from contextlib import ExitStack

import concourse.bass as bass
import concourse.mybir as mybir
from concourse.bass_utils import run_bass_kernel_spmd

F32 = mybir.dt.float32
BF16 = mybir.dt.bfloat16
AF = mybir.ActivationFunctionType
ALU = mybir.AluOpType
AXX = mybir.AxisListType.X

MAGIC = 12582912.0
K2SCALE = float(np.float32(0.02)) / 32.0
EPS_LN = 1e-5
EPS_KC = 1e-6
EPS_Q = 1e-5
D = 1024
INNER = 4096
DC = D // 128
DEBUG_ATTN = False


class Sem:

    def __init__(self, st, nc, name):
        self.h = st.enter_context(nc.semaphore(name))
        self.n = 0

    def inc(self, instr, k=1):
        instr.then_inc(self.h, k)
        self.n += k
        return instr

    def wait(self, eng, n=None):
        eng.wait_ge(self.h, self.n if n is None else n)


def build_nc(L=4096, T=512):
    NCH = L // T
    TT = T // 128
    assert T % 128 == 0 and L % T == 0 and TT == 4

    nc = bass.Bass("TRN2", target_bir_lowering=False)
    x_hbm = nc.dram_tensor("x", [L, D], F32, kind="ExternalInput")
    y_hbm = nc.dram_tensor("y", [L, D], F32, kind="ExternalOutput")
    wq1t = nc.dram_tensor("wq1t", [D, D], F32, kind="ExternalInput")
    wk1t = nc.dram_tensor("wk1t", [D, D], F32, kind="ExternalInput")
    wk2t = nc.dram_tensor("wk2t", [D, D], F32, kind="ExternalInput")
    wpt = nc.dram_tensor("wpt", [D, D], F32, kind="ExternalInput")
    wgt = nc.dram_tensor("wgt", [D, INNER], F32, kind="ExternalInput")
    wut = nc.dram_tensor("wut", [D, INNER], F32, kind="ExternalInput")
    wdt = nc.dram_tensor("wdt", [INNER, D], F32, kind="ExternalInput")
    bq1 = nc.dram_tensor("bq1", [D], F32, kind="ExternalInput")
    bk1 = nc.dram_tensor("bk1", [D], F32, kind="ExternalInput")
    bk2 = nc.dram_tensor("bk2", [D], F32, kind="ExternalInput")
    bpr = nc.dram_tensor("bpr", [D], F32, kind="ExternalInput")
    sc_g = nc.dram_tensor("sc_g", [8, 128, 8, 512], BF16, kind="Internal")
    sc_u = nc.dram_tensor("sc_u", [8, 128, 8, 512], BF16, kind="Internal")
    sc_d = nc.dram_tensor("sc_d", [8, 128, 4, 1024], BF16, kind="Internal")
    if DEBUG_ATTN:
        dbg = nc.dram_tensor("dbg", [6, 128, 4104], F32, kind="ExternalOutput")
        dbgq = nc.dram_tensor("dbgq", [2, 128, 4104], BF16, kind="ExternalOutput")
        dbg2 = nc.dram_tensor("dbg2", [128, 4104], F32, kind="ExternalOutput")

    st = ExitStack()
    E = st.enter_context
    xsb = [E(nc.sbuf_tensor(f"xsb{i}", [128, TT, D], F32)) for i in range(2)]
    pool = E(nc.sbuf_tensor("pool", [128, 25616], F32))
    r3 = lambda ap, a: ap.rearrange("p (a b) -> p a b", a=a)
    VT = r3(pool[:, 0:4096], 8)
    KB = r3(pool[:, 4096:8192], 8)
    QB = r3(pool[:, 8192:12296], 8)
    KV = r3(pool[:, 12296:16392], 8)
    OAR = r3(pool[:, 16392:20496], 8)
    K2B = r3(pool[:, 20496:24592], 8)
    silu2 = pool[:, 24592:25616]
    hbuf = r3(pool[:, 0:TT * 4096], TT)
    qstage = pool[:, 0:TT * 1024]
    tpiece = pool[:, 20496:24592]
    poolb = E(nc.sbuf_tensor("poolb", [128, 20488], BF16))
    aq = poolb[:, 0:4096].rearrange("p (a b c) -> p a b c", a=TT, b=DC)
    hq = poolb[:, 4096:20480].rearrange("p (a b c) -> p a b c", a=TT, b=32)
    xn2 = r3(poolb[:, 0:4096], TT)
    xn2T = r3(poolb[:, 4096:8200], 8)
    OB_ = r3(poolb[:, 8200:12296], 8)
    tstage = poolb[:, 12296:16392]
    awb = poolb[:, 12296:20488]
    sarena = E(nc.sbuf_tensor("sarena", [128, 12288], BF16))
    gsl = sarena[:, 0:4096]
    usl = sarena[:, 4096:8192]
    dsl = sarena[:, 8192:12288]
    ident = E(nc.sbuf_tensor("ident", [128, 128], F32))
    iota_p = pool[:, 0:128]
    iota_f = pool[:, 128:256]
    onesc = E(nc.sbuf_tensor("onesc", [128, 1], F32))
    onesr32 = E(nc.sbuf_tensor("onesr32", [1, 128], F32))
    onesrb = E(nc.sbuf_tensor("onesrb", [1, 128], BF16))
    bprow = E(nc.sbuf_tensor("bprow", [1, D], BF16))
    bsb = E(nc.sbuf_tensor("bsb", [128, 3 * DC], F32))
    smal = E(nc.sbuf_tensor("smal", [128, 288], F32))
    mean1 = E(nc.sbuf_tensor("mean1", [1, 8], F32))
    stats = smal[:, 0:48]
    mv = smal[:, 48:56]
    rstd = smal[:, 56:60]
    nmr = smal[:, 60:64]
    hi = smal[:, 64:68]
    lo = smal[:, 68:72]
    t1 = smal[:, 72:76]
    t2 = smal[:, 76:80]
    cmax = smal[:, 80:84]
    s1 = smal[:, 84:88]
    Aq = smal[:, 88:92]
    Bq = smal[:, 92:96]
    cg = smal[:, 96:100]
    cu = smal[:, 100:104]
    hmax = smal[:, 104:108]
    s2v = smal[:, 108:112]
    cdv = smal[:, 112:116]
    kc_carry = smal[:, 116:124]
    kv_carry = smal[:, 124:132]
    s2_carry = smal[:, 132:140]
    swb = smal[:, 140:143]
    asum = smal[:, 144:160]
    oarc = smal[:, 160:168]
    qc = smal[:, 168:176]
    xnc = smal[:, 176:184]
    preS = [smal[:, 192:224], smal[:, 224:256]]
    magic_ap = smal[:, 184:185]
    cmb = smal[:, 188:191]
    epsln_ap = smal[:, 185:186]
    epskc_ap = smal[:, 186:187]
    nmagic_ap = smal[:, 187:188]
    PB = E(nc.psum_tensor("PB", [128, 4096], F32))
    pb = [PB[:, k * 512:(k + 1) * 512] for k in range(8)]

    dX = [Sem(st, nc, "dX0"), Sem(st, nc, "dX1")]
    dO = Sem(st, nc, "dO")
    dGU = [Sem(st, nc, "dGU0"), Sem(st, nc, "dGU1")]
    dDN = [Sem(st, nc, "dDN0"), Sem(st, nc, "dDN1")]
    dTI = [Sem(st, nc, f"dTI{i}") for i in range(4)]
    dTG = [Sem(st, nc, f"dTG{i}") for i in range(4)]
    dTU = [Sem(st, nc, f"dTU{i}") for i in range(4)]
    dTD = [Sem(st, nc, f"dTD{i}") for i in range(4)]
    dAW = [Sem(st, nc, "dAW0"), Sem(st, nc, "dAW1"),
           Sem(st, nc, "dAW2"), Sem(st, nc, "dAW3")]
    dBP = Sem(st, nc, "dBP")
    dB = Sem(st, nc, "dB")
    dDB = Sem(st, nc, "dDB")
    cV = Sem(st, nc, "cV")
    cS = Sem(st, nc, "cS")
    cT = Sem(st, nc, "cT")
    cG = Sem(st, nc, "cG")

    V, S, TE, SY, G = nc.vector, nc.scalar, nc.tensor, nc.sync, nc.gpsimd

    G.iota(iota_p[:, :], [[0, 128]], channel_multiplier=1,
           allow_small_or_imprecise_dtypes=True)
    cG.inc(G.iota(iota_f[:, :], [[1, 128]], channel_multiplier=0,
                  allow_small_or_imprecise_dtypes=True))
    cG.wait(V)
    V.tensor_tensor(out=ident[:], in0=iota_p[:], in1=iota_f[:], op=ALU.is_equal)
    V.memset(onesc[:], 1.0)
    V.memset(onesr32[:], 1.0)
    V.memset(onesrb[:], 1.0)
    V.memset(kc_carry, 0.0)
    V.memset(kv_carry, 0.0)
    V.memset(s2_carry, 0.0)
    V.memset(oarc, 0.0)
    V.memset(qc, 0.0)
    V.memset(xnc, 0.0)
    V.memset(magic_ap, MAGIC)
    V.memset(epsln_ap, EPS_LN)
    V.memset(epskc_ap, EPS_KC)
    cV.inc(V.memset(nmagic_ap, -MAGIC))
    dB.inc(SY.dma_start(out=bsb[:, 0:DC],
                        in_=bq1.ap().rearrange("(j p) -> p j", p=128),
                        allow_slow_non_contiguous=True), 16)
    dB.inc(SY.dma_start(out=bsb[:, DC:2 * DC],
                        in_=bk1.ap().rearrange("(j p) -> p j", p=128),
                        allow_slow_non_contiguous=True), 16)
    dB.inc(SY.dma_start(out=bsb[:, 2 * DC:3 * DC],
                        in_=bk2.ap().rearrange("(j p) -> p j", p=128),
                        allow_slow_non_contiguous=True), 16)
    dBP.inc(G.dma_start(out=bprow[:, :], in_=bpr.ap()[None, :]), 16)
    b_q1 = bsb[:, 0:DC]
    b_k1 = bsb[:, DC:2 * DC]
    b_k2s = bsb[:, 2 * DC:3 * DC]
    dB.wait(V)
    cV.inc(V.tensor_scalar(out=b_k2s, in0=b_k2s, scalar1=K2SCALE, scalar2=None,
                           op0=ALU.mult))
    dX[0].inc(SY.dma_start(out=xsb[0][:, :, :],
                           in_=x_hbm.ap()[0:T, :]
                           .rearrange("(a p) d -> p a d", p=128)), 16)
    tern_done = {}

    ttr = smal[:, 191:192]
    tpc = [tpiece[:, 0:2048], tpiece[:, 2048:4096],
           pool[:, 16392:18440], pool[:, 18440:20488]]
    tst = [poolb[:, 12296:14344], poolb[:, 14344:16392],
           poolb[:, 8200:10248], poolb[:, 10248:12296]]
    tw_state = {"idx": 0, "slotV": [None] * 4, "stgV": [None] * 4}

    def tern_dma_in(src_ap, pf):
        i = tw_state["idx"]
        tw_state["idx"] += 1
        bsel = i % 4
        buf = tpc[bsel][:, 0:pf]
        if tw_state["slotV"][bsel] is not None:
            sem_, n_ = tw_state["slotV"][bsel]
            SY.wait_ge(sem_.h, n_)
        else:
            cV.wait(SY)
        dTI[bsel].inc(SY.dma_start(out=buf, in_=src_ap), 16)
        return bsel, buf

    def mean_piece(src_ap, pf, acc_slot):
        bsel, buf = tern_dma_in(src_ap, pf)
        dTI[bsel].wait(S)
        cS.inc(S.activation(out=buf, in_=buf, func=AF.Abs,
                            accum_out=acc_slot))
        tw_state["slotV"][bsel] = (cS, cS.n)

    def mean_finalize(sw_slot, inv_n, cm_slot, cmf):
        cS.wait(V)
        cV.inc(V.reduce_sum(out=ttr, in_=asum[:, :], axis=AXX))
        cV.wait(TE)
        cS.wait(TE)
        cT.inc(TE.matmul(out=pb[0][0:1, 0:1], lhsT=onesc[:, :], rhs=ttr,
                         start=True, stop=True))
        cT.wait(V)
        cV.inc(V.tensor_scalar(out=mean1[:, 0:1], in0=pb[0][0:1, 0:1],
                               scalar1=inv_n, scalar2=1e-5,
                               op0=ALU.mult, op1=ALU.max))
        cV.wait(V)
        V.reciprocal(out=mean1[:, 2:3], in_=mean1[:, 0:1])
        cV.inc(V.tensor_scalar(out=mean1[:, 3:4], in0=mean1[:, 0:1],
                               scalar1=cmf, scalar2=None, op0=ALU.mult))
        cV.wait(TE)
        cS.wait(TE)
        TE.matmul(out=pb[1][:, 0:1], lhsT=onesr32[0:1, :],
                  rhs=mean1[0:1, 2:3], start=True, stop=True)
        cT.inc(TE.matmul(out=pb[2][:, 0:1], lhsT=onesr32[0:1, :],
                         rhs=mean1[0:1, 3:4], start=True, stop=True))
        cT.wait(V)
        V.tensor_copy(out=sw_slot, in_=pb[1][:, 0:1])
        cV.inc(V.tensor_copy(out=cm_slot, in_=pb[2][:, 0:1]))

    def tern_mean(w_hbm, n_rows, n_free, sw_slot, inv_n, cm_slot, cmf):
        kchunks = n_rows // 128
        cV.wait(S)
        idx = 0
        if n_free >= 2048:
            for kc in range(kchunks):
                for j in range(n_free // 2048):
                    src = bass.AP(w_hbm, kc * 128 * n_free + j * 2048,
                                  [[n_free, 128], [1, 2048]])
                    mean_piece(src, 2048, asum[:, idx:idx + 1])
                    idx += 1
        else:
            pr = 2048 // n_free
            for kc in range(0, kchunks, pr):
                src = bass.AP(w_hbm, kc * 128 * n_free,
                              [[n_free, 128], [128 * n_free, pr], [1, n_free]])
                mean_piece(src, n_free * pr, asum[:, idx:idx + 1])
                idx += 1
        mean_finalize(sw_slot, inv_n, cm_slot, cmf)

    def tern_piece(src_ap, dst_ap, sw_slot, pf, osems, sw_wait=False):
        bsel, buf = tern_dma_in(src_ap, pf)
        sbf = tst[bsel][:, 0:pf]
        dTI[bsel].wait(V)
        cV.wait(V)
        cV.inc(V.tensor_scalar(out=buf, in0=buf, scalar1=sw_slot, scalar2=MAGIC,
                               op0=ALU.mult, op1=ALU.add))
        cV.wait(V)
        cV.inc(V.tensor_scalar(out=buf, in0=buf, scalar1=MAGIC, scalar2=1.0,
                               op0=ALU.subtract, op1=ALU.min))
        cV.wait(V)
        prev = tw_state["stgV"][bsel]
        if prev is not None:
            prev[0].wait(V, prev[1])
        cV.inc(V.tensor_scalar(out=sbf, in0=buf, scalar1=-1.0, scalar2=None,
                               op0=ALU.max))
        tw_state["slotV"][bsel] = (cV, cV.n)
        cV.wait(G)
        osem = osems[bsel]
        osem.inc(G.dma_start(out=dst_ap, in_=sbf), 16)
        tw_state["stgV"][bsel] = (osem, osem.n)

    def tern_write_gu(w_hbm, scratch, sw_slot, osems):
        for ob in range(8):
            for icg in range(2):
                src = bass.AP(w_hbm, icg * 4 * 128 * INNER + ob * 512,
                              [[INNER, 128], [128 * INNER, 4], [1, 512]])
                dst = bass.AP(scratch, ob * (128 * 8 * 512) + icg * 4 * 512,
                              [[8 * 512, 128], [512, 4], [1, 512]])
                tern_piece(src, dst, sw_slot, 2048, osems,
                           sw_wait=(ob == 0 and icg == 0))

    def tern_write_d(w_hbm, scratch, sw_slot, osems):
        for icg in range(8):
            for h2 in range(2):
                src = bass.AP(w_hbm, (icg * 4 + h2 * 2) * 128 * D,
                              [[D, 128], [128 * D, 2], [1, D]])
                dst = bass.AP(scratch, icg * (128 * 4 * 1024) + h2 * 2 * 1024,
                              [[4 * 1024, 128], [1024, 2], [1, 1024]])
                tern_piece(src, dst, sw_slot, 2048, osems,
                           sw_wait=(icg == 0 and h2 == 0))

    def emit_tern_stream():
        tern_mean(wgt, D, INNER, swb[:, 0:1], 1.0 / (D * INNER), cmb[:, 0:1],
                  1.0 / 127.0)
        tern_write_gu(wgt, sc_g, swb[:, 0:1], dTG)
        tern_mean(wut, D, INNER, swb[:, 1:2], 1.0 / (D * INNER), cmb[:, 1:2],
                  1.0 / 127.0)
        tern_write_gu(wut, sc_u, swb[:, 1:2], dTU)
        tern_mean(wdt, INNER, D, swb[:, 2:3], 1.0 / (D * INNER), cmb[:, 2:3],
                  1.0 / 254.0)
        tern_write_d(wdt, sc_d, swb[:, 2:3], dTD)

    def ln_stats_part(xb, tts):
        cV.wait(V)
        last = None
        for tt in tts:
            V.bn_stats(out=stats[:, tt * 12:tt * 12 + 6], in_=xb[:, tt, 0:512])
            V.bn_stats(out=stats[:, tt * 12 + 6:tt * 12 + 12],
                       in_=xb[:, tt, 512:1024])
            V.reduce_max(out=hi[:, tt:tt + 1], in_=xb[:, tt, :], axis=AXX)
            last = V.tensor_reduce(out=lo[:, tt:tt + 1], in_=xb[:, tt, :],
                                   axis=AXX, op=ALU.min)
        cV.inc(last)

    def ln_stats(xb, skip01=False):
        ln_stats_part(xb, (2, 3) if skip01 else (0, 1, 2, 3))
        cV.wait(V)
        for tt in range(TT - 1):
            V.bn_aggr(out=mv[:, tt * 2:tt * 2 + 2],
                      in_=stats[:, tt * 12:tt * 12 + 12])
        tt = TT - 1
        cV.inc(V.bn_aggr(out=mv[:, tt * 2:tt * 2 + 2],
                         in_=stats[:, tt * 12:tt * 12 + 12]))

    def ln_finalize():
        cV.wait(S)
        cS.inc(S.activation(out=rstd, in_=mv[:, 1:8:2], func=AF.Ln, bias=epsln_ap))
        cS.wait(S)
        cS.inc(S.activation(out=rstd, in_=rstd, func=AF.Exp, scale=-0.5))

    def quant_scales():
        cS.wait(V)
        V.tensor_tensor(out=t1, in0=hi, in1=mv[:, 0:8:2], op=ALU.subtract)
        cV.inc(V.tensor_tensor(out=t2, in0=mv[:, 0:8:2], in1=lo, op=ALU.subtract))
        cV.wait(V)
        cV.inc(V.tensor_tensor(out=cmax, in0=t1, in1=t2, op=ALU.max))
        cV.wait(V)
        cV.inc(V.tensor_tensor(out=t1, in0=rstd, in1=cmax, op=ALU.mult))
        cV.wait(V)
        cV.inc(V.tensor_scalar(out=t1, in0=t1, scalar1=EPS_Q, scalar2=None,
                               op0=ALU.max))
        cV.wait(V)
        cV.inc(V.reciprocal(out=t2, in_=t1))
        cV.wait(V)
        cV.inc(V.scalar_tensor_tensor(out=Aq, in0=t2, scalar=127.0, in1=rstd,
                                      op0=ALU.mult, op1=ALU.mult))
        cV.wait(V)
        cV.inc(V.tensor_tensor(out=Bq, in0=mv[:, 0:8:2], in1=Aq, op=ALU.mult))
        cV.wait(V)
        cV.inc(V.tensor_scalar(out=Bq, in0=Bq, scalar1=-1.0, scalar2=MAGIC,
                               op0=ALU.mult, op1=ALU.add))

    def transpose_quads(quads, evac="copy", eng="alt"):
        savedV = [cV.n] * 4
        savedS = [cS.n] * 4
        for qi, (srcs, dst) in enumerate(quads):
            b = qi % 4
            TE.wait_ge(cV.h, savedV[b])
            TE.wait_ge(cS.h, savedS[b])
            for j, s_ in enumerate(srcs[:-1]):
                TE.transpose(out=PB[:, b * 1024 + j * 128:
                                    b * 1024 + (j + 1) * 128],
                             in_=s_, identity=ident[:, :])
            cT.inc(TE.transpose(
                out=PB[:, b * 1024 + (len(srcs) - 1) * 128:
                       b * 1024 + len(srcs) * 128],
                in_=srcs[-1], identity=ident[:, :]))
            pt = PB[:, b * 1024:b * 1024 + len(srcs) * 128]
            if eng == "V" or (eng == "alt" and qi % 2 == 0):
                cT.wait(V)
                if evac == "copy":
                    cV.inc(V.tensor_copy(out=dst, in_=pt))
                else:
                    cV.inc(V.tensor_scalar(out=dst, in0=pt, scalar1=MAGIC,
                                           scalar2=None, op0=ALU.subtract))
                savedV[b] = cV.n
            else:
                cT.wait(S)
                if evac == "copy":
                    cS.inc(S.copy(out=dst, in_=pt))
                else:
                    cS.inc(S.add(out=dst, in_=pt, add=nmagic_ap))
                savedS[b] = cS.n

    def xstage(tt):
        hs, ti = tt // 2, tt % 2
        return VT[:, :, hs * 256 + ti * 128:hs * 256 + ti * 128 + 128]

    def attn_pre_tts(xb, tts):
        ln_stats_part(xb, tts)
        cV.wait(V)
        last = None
        for tt in tts:
            last = V.bn_aggr(out=mv[:, tt * 2:tt * 2 + 2],
                             in_=stats[:, tt * 12:tt * 12 + 12])
        cV.inc(last)
        t0 = tts[0]
        cV.wait(S)
        cS.inc(S.activation(out=rstd[:, t0:t0 + 2],
                            in_=mv[:, 2 * t0 + 1:2 * t0 + 4:2],
                            func=AF.Ln, bias=epsln_ap))
        cS.wait(S)
        cS.inc(S.activation(out=rstd[:, t0:t0 + 2], in_=rstd[:, t0:t0 + 2],
                            func=AF.Exp, scale=-0.5))
        cS.wait(V)
        cV.inc(V.scalar_tensor_tensor(out=nmr[:, t0:t0 + 2],
                                      in0=mv[:, 2 * t0:2 * t0 + 4:2],
                                      scalar=-1.0, in1=rstd[:, t0:t0 + 2],
                                      op0=ALU.mult, op1=ALU.mult))
        cV.wait(S)
        sl_ = None
        for tt in tts:
            sl_ = S.activation(out=xstage(tt), in_=xb[:, tt, :],
                               func=AF.Identity, scale=rstd[:, tt:tt + 1],
                               bias=nmr[:, tt:tt + 1])
        cS.inc(sl_)

    def emit_ffn_pre(c, xb, first_of_chunk, p, skip01=False):
        ps = preS[p]
        pmv, prstd = ps[:, 0:8], ps[:, 8:12]
        pAq, pBq = ps[:, 12:16], ps[:, 16:20]
        pcg, pcu = ps[:, 20:24], ps[:, 24:28]
        pt1 = ps[:, 28:32]
        if first_of_chunk:
            dX[c % 2].wait(V)
        ln_stats(xb, skip01=skip01)
        cV.wait(S)
        cS.inc(S.activation(out=prstd, in_=mv[:, 1:8:2], func=AF.Ln,
                            bias=epsln_ap))
        cS.wait(S)
        cS.inc(S.activation(out=prstd, in_=prstd, func=AF.Exp, scale=-0.5))
        cS.wait(V)
        cV.inc(V.tensor_copy(out=pmv, in_=mv))
        cV.wait(V)
        V.tensor_tensor(out=t1, in0=hi, in1=mv[:, 0:8:2], op=ALU.subtract)
        cV.inc(V.tensor_tensor(out=t2, in0=mv[:, 0:8:2], in1=lo,
                               op=ALU.subtract))
        cV.wait(V)
        cV.inc(V.tensor_tensor(out=cmax, in0=t1, in1=t2, op=ALU.max))
        cV.wait(V)
        cV.inc(V.tensor_tensor(out=pt1, in0=prstd, in1=cmax, op=ALU.mult))
        cV.wait(V)
        cV.inc(V.tensor_scalar(out=pt1, in0=pt1, scalar1=EPS_Q, scalar2=None,
                               op0=ALU.max))
        cV.wait(V)
        cV.inc(V.reciprocal(out=t2, in_=pt1))
        cV.wait(V)
        cV.inc(V.scalar_tensor_tensor(out=pAq, in0=t2, scalar=127.0, in1=prstd,
                                      op0=ALU.mult, op1=ALU.mult))
        cV.wait(V)
        cV.inc(V.tensor_tensor(out=pBq, in0=pmv[:, 0:8:2], in1=pAq,
                               op=ALU.mult))
        cV.wait(V)
        cV.inc(V.tensor_scalar(out=pBq, in0=pBq, scalar1=-1.0, scalar2=MAGIC,
                               op0=ALU.mult, op1=ALU.add))
        cV.wait(S)
        sl_ = None
        for tt in range(TT):
            sl_ = S.activation(out=qstage[:, tt * 1024:(tt + 1) * 1024],
                               in_=xb[:, tt, :], func=AF.Identity,
                               scale=pAq[:, tt:tt + 1], bias=pBq[:, tt:tt + 1])
        cS.inc(sl_)
        cS.wait(TE)
        transpose_quads(
            [([qstage[:, tt * 1024 + ic * 128:tt * 1024 + (ic + 1) * 128]
               for ic in range(DC)],
              aq[:, tt, :, :]) for tt in range(TT)], evac="submagic")

    def emit_ffn_rest(c, xb, p, pre_next=None, attn_next=None):
        cg = preS[p][:, 20:24]
        cu = preS[p][:, 24:28]
        pt1 = preS[p][:, 28:32]
        cV.wait(V)
        V.tensor_scalar(out=cg, in0=pt1, scalar1=cmb[:, 0:1], scalar2=None,
                        op0=ALU.mult)
        cV.inc(V.tensor_scalar(out=cu, in0=pt1, scalar1=cmb[:, 1:2],
                               scalar2=None, op0=ALU.mult))
        gu_arena_cT = [cT.n, cT.n]
        evac_saved = [(cV.n, cS.n), (cV.n, cS.n)]
        slot_cV = [cV.n, cV.n]

        def gu_load(k):
            ob_, icg_ = divmod(k, 2)
            bs = k % 2
            S.wait_ge(cT.h, gu_arena_cT[bs])
            src_g = bass.AP(sc_g, ob_ * (128 * 8 * 512) + icg_ * 4 * 512,
                            [[8 * 512, 128], [512, 4], [1, 512]])
            src_u = bass.AP(sc_u, ob_ * (128 * 8 * 512) + icg_ * 4 * 512,
                            [[8 * 512, 128], [512, 4], [1, 512]])
            dGU[bs].inc(S.dma_start(
                out=gsl[:, bs * 2048:(bs + 1) * 2048], in_=src_g), 16)
            dGU[bs].inc(S.dma_start(
                out=usl[:, bs * 2048:(bs + 1) * 2048], in_=src_u), 16)

        if c == 0:
            for s_, n_ in tern_done["gu"]:
                s_.wait(S, n_)
        gu_load(0)
        gu_load(1)

        hpart = smal[:, 256:288]

        def gu_evac(ob_, ttg):
            cT.wait(S)
            cT.wait(V)
            for tt in (ttg * 2, ttg * 2 + 1):
                slot = tt % 2
                sl = silu2[:, slot * 512:slot * 512 + 512]
                S.wait_ge(cV.h, slot_cV[slot])
                cS.inc(S.activation(out=sl, in_=pb[ttg * 4 + (tt % 2)][:, :],
                                    func=AF.Silu, scale=cg[:, tt:tt + 1]))
                cS.wait(V)
                cV.inc(V.scalar_tensor_tensor(
                    out=hbuf[:, tt, ob_ * 512:(ob_ + 1) * 512],
                    in0=pb[ttg * 4 + 2 + (tt % 2)][:, :],
                    scalar=cu[:, tt:tt + 1], in1=sl,
                    op0=ALU.mult, op1=ALU.mult))
                cV.wait(V)
                cV.inc(V.reduce_max(out=hpart[:, tt * 8 + ob_:tt * 8 + ob_ + 1],
                                    in_=hbuf[:, tt, ob_ * 512:(ob_ + 1) * 512],
                                    axis=AXX, apply_absolute_value=True))
                slot_cV[slot] = cV.n
            evac_saved[ttg] = (cV.n, cS.n)

        first_mm = True
        for ob in range(8):
            for icg in range(2):
                k = ob * 2 + icg
                bsel = k % 2
                dGU[bsel].wait(TE)
                if first_mm:
                    cV.wait(TE)
                    cS.wait(TE)
                    first_mm = False
                for ttg in range(2):
                    if icg == 0:
                        TE.wait_ge(cV.h, evac_saved[ttg][0])
                        TE.wait_ge(cS.h, evac_saved[ttg][1])
                    tl_ = None
                    for ic4 in range(4):
                        ic = icg * 4 + ic4
                        for tt in (ttg * 2, ttg * 2 + 1):
                            TE.matmul(out=pb[ttg * 4 + (tt % 2)][:, :],
                                      lhsT=aq[:, tt, ic, :],
                                      rhs=gsl[:, bsel * 2048 + ic4 * 512:
                                              bsel * 2048 + (ic4 + 1) * 512],
                                      start=(ic == 0), stop=(ic == 7))
                            tl_ = TE.matmul(
                                out=pb[ttg * 4 + 2 + (tt % 2)][:, :],
                                lhsT=aq[:, tt, ic, :],
                                rhs=usl[:, bsel * 2048 + ic4 * 512:
                                        bsel * 2048 + (ic4 + 1) * 512],
                                start=(ic == 0), stop=(ic == 7))
                    cT.inc(tl_)
                    if icg == 1:
                        gu_evac(ob, ttg)
                gu_arena_cT[bsel] = cT.n
                if k + 2 < 16:
                    gu_load(k + 2)
        cV.wait(V)
        if c == 0:
            for s_, n_ in tern_done["d"]:
                s_.wait(V, n_)
                s_.wait(S, n_)
        for tt in range(TT):
            cV.inc(V.reduce_max(out=hmax[:, tt:tt + 1],
                                in_=hpart[:, tt * 8:tt * 8 + 8], axis=AXX))
            cV.wait(V)
            cV.inc(V.tensor_scalar(out=hmax[:, tt:tt + 1],
                                   in0=hmax[:, tt:tt + 1], scalar1=EPS_Q,
                                   scalar2=None, op0=ALU.max))
            cV.wait(V)
            V.reciprocal(out=t1[:, tt:tt + 1], in_=hmax[:, tt:tt + 1])
            cV.inc(V.tensor_scalar(out=cdv[:, tt:tt + 1],
                                   in0=hmax[:, tt:tt + 1],
                                   scalar1=cmb[:, 2:3], scalar2=None,
                                   op0=ALU.mult))
            cV.wait(V)
            cV.inc(V.tensor_scalar(out=s2v[:, tt:tt + 1], in0=t1[:, tt:tt + 1],
                                   scalar1=127.0, scalar2=None, op0=ALU.mult))
            cV.wait(S)
            cS.inc(S.activation(out=hbuf[:, tt, :], in_=hbuf[:, tt, :],
                                func=AF.Identity, scale=s2v[:, tt:tt + 1],
                                bias=magic_ap))
            cS.wait(TE)
            transpose_quads(
                [([hbuf[:, tt, ic * 128:(ic + 1) * 128]
                   for ic in range(q * 8, q * 8 + 8)],
                  hq[:, tt, q * 8:q * 8 + 8, :]) for q in range(4)],
                evac="submagic")
        if pre_next is not None:
            pre_next()
        dn_arena_cT = [cT.n, cT.n]

        def dn_load(ih_):
            bs = ih_ % 2
            S.wait_ge(cT.h, dn_arena_cT[bs])
            src_d = bass.AP(sc_d,
                            (ih_ // 2) * (128 * 4 * 1024) + (ih_ % 2) * 2048,
                            [[4 * 1024, 128], [1024, 2], [1, 1024]])
            dDN[bs].inc(S.dma_start(out=dsl[:, bs * 2048:(bs + 1) * 2048],
                                    in_=src_d), 16)

        if c == 0:
            for s_, n_ in tern_done["d"]:
                s_.wait(S, n_)
        dn_load(0)
        dn_load(1)
        for ih in range(16):
            bsel = ih % 2
            dDN[bsel].wait(TE)
            if ih == 0:
                cV.wait(TE)
                cS.wait(TE)
            tl_ = None
            for ic2 in range(2):
                ic = ih * 2 + ic2
                for tt in range(TT):
                    for obk in range(2):
                        tl_ = TE.matmul(
                            out=pb[tt * 2 + obk][:, :], lhsT=hq[:, tt, ic, :],
                            rhs=dsl[:, bsel * 2048 + ic2 * 1024 + obk * 512:
                                    bsel * 2048 + ic2 * 1024 + (obk + 1) * 512],
                            start=(ic == 0), stop=(ic == 31))
            cT.inc(tl_)
            dn_arena_cT[bsel] = cT.n
            if ih + 2 < 16:
                dn_load(ih + 2)
        cT.wait(V)
        vl_ = None
        for tt in (0, 1):
            for obk in range(2):
                vl_ = V.scalar_tensor_tensor(
                    out=xb[:, tt, obk * 512:(obk + 1) * 512],
                    in0=pb[tt * 2 + obk][:, :], scalar=cdv[:, tt:tt + 1],
                    in1=xb[:, tt, obk * 512:(obk + 1) * 512],
                    op0=ALU.mult, op1=ALU.add)
        cV.inc(vl_)
        if attn_next is not None:
            attn_next()
        cV.wait(V)
        vl_ = None
        for tt in (2, 3):
            for obk in range(2):
                vl_ = V.scalar_tensor_tensor(
                    out=xb[:, tt, obk * 512:(obk + 1) * 512],
                    in0=pb[tt * 2 + obk][:, :], scalar=cdv[:, tt:tt + 1],
                    in1=xb[:, tt, obk * 512:(obk + 1) * 512],
                    op0=ALU.mult, op1=ALU.add)
        cV.inc(vl_)

    kc_mid = smal[:, 144:152]
    kv_mid = smal[:, 152:160]
    s2_mid = smal[:, 40:48]

    def emit_attn(c, xb, pre01_done=False):
        HT = T // 2
        if not pre01_done:
            attn_pre_tts(xb, (0, 1))
        cS.wait(TE)
        cS.wait(V)
        aw_arena_cT = [cT.n, cT.n, cT.n, cT.n]
        proj_evac_cS = [cS.n, cS.n]

        def aw_load(w_hbm, k):
            bs = k % 4
            G.wait_ge(cT.h, aw_arena_cT[bs])
            src = bass.AP(w_hbm, (k % 4) * 2 * 128 * D,
                          [[D, 128], [128 * D, 2], [1, D]])
            dAW[bs].inc(G.dma_start(
                out=awb[:, bs * 2048:(bs + 1) * 2048], in_=src), 16)

        def aw_tile(k, et):
            bs = (k // 2) % 4
            off = bs * 2048 + (k % 2) * 1024 + et * 128
            return awb[:, off:off + 128]

        def proj(w_hbm, out_fm, out_off, func, bias_ap, scale, rhs_off, th):
            aw_load(w_hbm, 0)
            aw_load(w_hbm, 1)
            for dc in range(DC):
                kp = dc // 2
                bsel = kp % 4
                dAW[bsel].wait(TE)
                if dc == 0:
                    TE.wait_ge(cS.h, proj_evac_cS[0])
                    TE.wait_ge(cS.h, proj_evac_cS[1])
                tl_ = None
                for et in range(DC):
                    tl_ = TE.matmul(
                        out=pb[et][:, 0:HT],
                        lhsT=aw_tile(dc, et),
                        rhs=xn2T[:, dc, rhs_off + th * HT:
                                 rhs_off + th * HT + HT],
                        start=(dc == 0), stop=(dc == DC - 1))
                cT.inc(tl_)
                if dc % 2 == 1:
                    aw_arena_cT[bsel] = cT.n
                    if kp + 2 < 4:
                        aw_load(w_hbm, kp + 2)
            for eh in range(2):
                cT.wait(S)
                sl2 = None
                for ei in range(4):
                    et = eh * 4 + ei
                    sl2 = S.activation(
                        out=out_fm[:, et, out_off + th * HT:
                                   out_off + th * HT + HT],
                        in_=pb[eh * 4 + ei][:, 0:HT], func=func,
                        bias=bias_ap[:, et:et + 1], scale=scale)
                cS.inc(sl2)
                proj_evac_cS[eh] = cS.n

        def emit_half_mms(th):
            tt0 = th * 2
            transpose_quads(
                [([xstage(tt)[:, dc, :] for dc in range(DC)],
                  xn2T[:, :, 1 + tt * 128:1 + (tt + 1) * 128])
                 for tt in (tt0, tt0 + 1)])
            transpose_quads(
                [([xb[:, tt, dc * 128:(dc + 1) * 128] for dc in range(DC)],
                  VT[:, :, tt * 128:(tt + 1) * 128])
                 for tt in (tt0, tt0 + 1)])
            if th == 0:
                cV.wait(V)
                V.tensor_copy(out=xn2T[:, :, 0], in_=xnc)
                V.tensor_copy(out=QB[:, :, 0], in_=qc)
                cV.inc(V.tensor_copy(out=OAR[:, :, 0], in_=oarc))
            cV.wait(TE)
            cS.wait(TE)
            proj(wk1t, KB, 0, AF.Exp, b_k1, 1.0, 1, th)
            proj(wq1t, QB, 1, AF.Sigmoid, b_q1, 1.0, 1, th)
            proj(wk2t, K2B, 0, AF.Sigmoid, b_k2s, K2SCALE, 0, th)
            half_cS[th] = cS.n
            half_cV[th] = cV.n

        def chain(th):
            o0 = th * HT
            cV_pre = cV.n
            lane_done = [None] * 4
            GL = (2, 3)

            def lslices(l):
                dl = slice(2 * l, 2 * l + 2)
                return (dl, KB[:, dl, o0:o0 + HT], KV[:, dl, o0:o0 + HT],
                        VT[:, dl, o0:o0 + HT], K2B[:, dl, o0:o0 + HT])

            G.wait_ge(cS.h, half_cS[th])
            G.wait_ge(cV.h, cV_pre)
            V.wait_ge(cS.h, half_cS[th])
            gkv = {}
            for l in GL:
                dl, KBl, KVl, VTl, K2l = lslices(l)
                cG.wait(G)
                cG.inc(G.tensor_tensor(out=KVl, in0=KBl, in1=VTl,
                                       op=ALU.mult))
                gkv[l] = cG.n
            kc_done, kvc_done, rkc_done = {}, {}, {}
            for l in GL:
                dl = slice(2 * l, 2 * l + 2)
                V.wait_ge(cG.h, gkv[l])
                cV.wait(V)
                for dc in (2 * l, 2 * l + 1):
                    ini = (kc_carry[:, dc:dc + 1] if th == 0
                           else kc_mid[:, dc:dc + 1])
                    ll = V.tensor_tensor_scan(out=KB[:, dc, o0:o0 + HT],
                                              data0=KB[:, dc, o0:o0 + HT],
                                              data1=KB[:, dc, o0:o0 + HT],
                                              initial=ini,
                                              op0=ALU.add, op1=ALU.bypass)
                cV.inc(ll)
                kc_done[l] = cV.n
            for l in GL:
                dl = slice(2 * l, 2 * l + 2)
                cV.wait(V)
                for dc in (2 * l, 2 * l + 1):
                    ini = (kv_carry[:, dc:dc + 1] if th == 0
                           else kv_mid[:, dc:dc + 1])
                    ll = V.tensor_tensor_scan(out=KV[:, dc, o0:o0 + HT],
                                              data0=KV[:, dc, o0:o0 + HT],
                                              data1=KV[:, dc, o0:o0 + HT],
                                              initial=ini,
                                              op0=ALU.add, op1=ALU.bypass)
                cV.inc(ll)
                cV.wait(V)
                mdst = (kc_mid, kv_mid) if th == 0 else (kc_carry, kv_carry)
                V.tensor_copy(out=mdst[0][:, dl], in_=KB[:, dl, o0 + HT - 1])
                cV.inc(V.tensor_copy(out=mdst[1][:, dl],
                                     in_=KV[:, dl, o0 + HT - 1]))
                kvc_done[l] = cV.n
            for l in GL:
                dl, KBl, KVl, VTl, K2l = lslices(l)
                S.wait_ge(cV.h, kvc_done[l])
                cS.wait(S)
                cS.inc(S.activation(out=KBl, in_=KBl, func=AF.Ln,
                                    bias=epskc_ap))
                cS.wait(S)
                cS.inc(S.activation(out=KBl, in_=KBl, func=AF.Exp,
                                    scale=-1.0))
                rkc_done[l] = cS.n

            for l in (0, 1):
                dl, KBl, KVl, VTl, K2l = lslices(l)
                cV.wait(V)
                cV.inc(V.tensor_tensor(out=KVl, in0=KBl, in1=VTl,
                                       op=ALU.mult))
                cV.wait(V)
                for dc in (2 * l, 2 * l + 1):
                    ini = (kc_carry[:, dc:dc + 1] if th == 0
                           else kc_mid[:, dc:dc + 1])
                    ll = V.tensor_tensor_scan(out=KB[:, dc, o0:o0 + HT],
                                              data0=KB[:, dc, o0:o0 + HT],
                                              data1=KB[:, dc, o0:o0 + HT],
                                              initial=ini,
                                              op0=ALU.add, op1=ALU.bypass)
                cV.inc(ll)
                cV.wait(V)
                for dc in (2 * l, 2 * l + 1):
                    ini = (kv_carry[:, dc:dc + 1] if th == 0
                           else kv_mid[:, dc:dc + 1])
                    ll = V.tensor_tensor_scan(out=KV[:, dc, o0:o0 + HT],
                                              data0=KV[:, dc, o0:o0 + HT],
                                              data1=KV[:, dc, o0:o0 + HT],
                                              initial=ini,
                                              op0=ALU.add, op1=ALU.bypass)
                cV.inc(ll)
                cV.wait(V)
                mdst = (kc_mid, kv_mid) if th == 0 else (kc_carry, kv_carry)
                V.tensor_copy(out=mdst[0][:, dl], in_=KB[:, dl, o0 + HT - 1])
                cV.inc(V.tensor_copy(out=mdst[1][:, dl],
                                     in_=KV[:, dl, o0 + HT - 1]))
                cV.wait(S)
                cS.inc(S.activation(out=KBl, in_=KBl, func=AF.Ln,
                                    bias=epskc_ap))
                cS.wait(S)
                cS.inc(S.activation(out=KBl, in_=KBl, func=AF.Exp,
                                    scale=-1.0))
                cS.wait(V)
                cV.wait(V)
                cV.inc(V.tensor_tensor(out=KVl, in0=KVl, in1=KBl,
                                       op=ALU.mult))
                cV.wait(V)
                cV.inc(V.tensor_tensor(out=OAR[:, dl, o0 + 1:o0 + HT + 1],
                                       in0=KVl,
                                       in1=QB[:, dl, o0 + 1:o0 + HT + 1],
                                       op=ALU.mult))
                cV.wait(V)
                cV.inc(V.tensor_tensor(out=KBl, in0=VTl,
                                       in1=OAR[:, dl, o0:o0 + HT],
                                       op=ALU.subtract))
                if c == 0 and th == 0:
                    cV.wait(V)
                    cV.inc(V.memset(KB[:, dl, 0:1], 0.0))
                cV.wait(V)
                cV.inc(V.tensor_tensor(out=KVl, in0=K2l, in1=KBl,
                                       op=ALU.mult))
                cV.wait(V)
                for dc in (2 * l, 2 * l + 1):
                    ini = (s2_carry[:, dc:dc + 1] if th == 0
                           else s2_mid[:, dc:dc + 1])
                    ll = V.tensor_tensor_scan(out=KV[:, dc, o0:o0 + HT],
                                              data0=KV[:, dc, o0:o0 + HT],
                                              data1=KV[:, dc, o0:o0 + HT],
                                              initial=ini,
                                              op0=ALU.add, op1=ALU.bypass)
                cV.inc(ll)
                cV.wait(V)
                mdst = s2_mid if th == 0 else s2_carry
                cV.inc(V.tensor_copy(out=mdst[:, dl],
                                     in_=KV[:, dl, o0 + HT - 1]))
                cV.wait(V)
                cV.inc(V.scalar_tensor_tensor(out=VTl, in0=KVl,
                                              scalar=K2SCALE,
                                              in1=QB[:, dl, o0:o0 + HT],
                                              op0=ALU.mult, op1=ALU.mult))
                cV.wait(V)
                cV.inc(V.tensor_tensor(out=OB_[:, dl, o0:o0 + HT], in0=VTl,
                                       in1=OAR[:, dl, o0 + 1:o0 + HT + 1],
                                       op=ALU.add))
                lane_done[l] = (cV, cV.n)
            k2e_done = {}
            for l in GL:
                dl, KBl, KVl, VTl, K2l = lslices(l)
                G.wait_ge(cV.h, kvc_done[l])
                G.wait_ge(cS.h, rkc_done[l])
                cG.wait(G)
                cG.inc(G.tensor_tensor(out=KVl, in0=KVl, in1=KBl,
                                       op=ALU.mult))
                cG.wait(G)
                cG.inc(G.tensor_tensor(out=OAR[:, dl, o0 + 1:o0 + HT + 1],
                                       in0=KVl,
                                       in1=QB[:, dl, o0 + 1:o0 + HT + 1],
                                       op=ALU.mult))
                cG.wait(G)
                cG.inc(G.tensor_tensor(out=KBl, in0=VTl,
                                       in1=OAR[:, dl, o0:o0 + HT],
                                       op=ALU.subtract))
                if c == 0 and th == 0:
                    cG.wait(G)
                    cG.inc(G.memset(KB[:, dl, 0:1], 0.0))
                cG.wait(G)
                cG.inc(G.tensor_tensor(out=KVl, in0=K2l, in1=KBl,
                                       op=ALU.mult))
                k2e_done[l] = cG.n

            oma_done = {}
            for l in GL:
                dl, KBl, KVl, VTl, K2l = lslices(l)
                V.wait_ge(cG.h, k2e_done[l])
                cV.wait(V)
                for dc in (2 * l, 2 * l + 1):
                    ini = (s2_carry[:, dc:dc + 1] if th == 0
                           else s2_mid[:, dc:dc + 1])
                    ll = V.tensor_tensor_scan(out=KV[:, dc, o0:o0 + HT],
                                              data0=KV[:, dc, o0:o0 + HT],
                                              data1=KV[:, dc, o0:o0 + HT],
                                              initial=ini,
                                              op0=ALU.add, op1=ALU.bypass)
                cV.inc(ll)
                cV.wait(V)
                mdst = s2_mid if th == 0 else s2_carry
                cV.inc(V.tensor_copy(out=mdst[:, dl],
                                     in_=KV[:, dl, o0 + HT - 1]))
                cV.wait(V)
                cV.inc(V.scalar_tensor_tensor(out=VTl, in0=KVl,
                                              scalar=K2SCALE,
                                              in1=QB[:, dl, o0:o0 + HT],
                                              op0=ALU.mult, op1=ALU.mult))
                oma_done[l] = cV.n
            for l in GL:
                dl = slice(2 * l, 2 * l + 2)
                G.wait_ge(cV.h, oma_done[l])
                cG.wait(G)
                cG.inc(G.tensor_tensor(out=OB_[:, dl, o0:o0 + HT],
                                       in0=VT[:, dl, o0:o0 + HT],
                                       in1=OAR[:, dl, o0 + 1:o0 + HT + 1],
                                       op=ALU.add))
                lane_done[l] = (cG, cG.n)
            if th == 1:
                cV.wait(V)
                cG.wait(V)
                V.tensor_copy(out=oarc, in_=OAR[:, :, 512])
                V.tensor_copy(out=qc, in_=QB[:, :, 512])
                cV.inc(V.tensor_copy(out=xnc, in_=xn2T[:, :, 512]))
            return lane_done

        def wproj_mms(th, lane_done):
            TE.wait_ge(cV.h, wp_evac_cV[th])
            TE.wait_ge(cS.h, proj_evac_cS[th])
            TE.wait_ge(lane_done[0][0].h, lane_done[0][1])
            for tt in (th * 2, th * 2 + 1):
                for eb in range(2):
                    TE.matmul(out=pb[th * 4 + (tt % 2) * 2 + eb][:, :],
                              lhsT=onesrb[:, :],
                              rhs=bprow[:, eb * 512:(eb + 1) * 512],
                              start=True, stop=False)
            for dc in range(DC):
                kp = dc // 2
                bsel = kp % 4
                dAW[bsel].wait(TE)
                if dc % 2 == 0:
                    TE.wait_ge(lane_done[kp][0].h, lane_done[kp][1])
                tl_ = None
                for tt in (th * 2, th * 2 + 1):
                    for eb in range(2):
                        tl_ = TE.matmul(
                            out=pb[th * 4 + (tt % 2) * 2 + eb][:, :],
                            lhsT=OB_[:, dc, tt * 128:(tt + 1) * 128],
                            rhs=aw_tile(dc, 0)[:, 0:0 + 512] if False else
                            awb[:, bsel * 2048 + (dc % 2) * 1024 + eb * 512:
                                bsel * 2048 + (dc % 2) * 1024 + (eb + 1) * 512],
                            start=False, stop=(dc == DC - 1))
                cT.inc(tl_)
                if dc % 2 == 1:
                    aw_arena_cT[bsel] = cT.n
                    if kp + 2 < 4:
                        aw_load(wpt, kp + 2)

        def wproj_evac(th):
            cT.wait(V)
            vl_ = None
            for tt in (th * 2, th * 2 + 1):
                for eb in range(2):
                    vl_ = V.tensor_tensor(
                        out=xb[:, tt, eb * 512:(eb + 1) * 512],
                        in0=pb[th * 4 + (tt % 2) * 2 + eb][:, :],
                        in1=xb[:, tt, eb * 512:(eb + 1) * 512],
                        op=ALU.add)
            cV.inc(vl_)
            wp_evac_cV[th] = cV.n

        wp_evac_cV = [cV.n, cV.n]
        half_cS = [cS.n, cS.n]
        half_cV = [cV.n, cV.n]
        if c == 0:
            for grp in (tern_done["gu"], tern_done["d"]):
                for s_, n_ in grp:
                    s_.wait(G, n_)
        emit_half_mms(0)
        attn_pre_tts(xb, (2, 3))
        emit_half_mms(1)
        if DEBUG_ATTN and c == 0:
            cS.wait(SY)
            cV.wait(SY)
            dDB.inc(SY.dma_start(out=bass.AP(dbg2, 0, [[4104, 128], [1, 4104]]),
                                 in_=pool[:, 8192:12296]), 16)
            dDB.wait(V)
            dDB.wait(S)
            dDB.wait(TE)
        if c == 0:
            for s_, n_ in tern_done["ti"] + tern_done["gu"] + tern_done["d"]:
                s_.wait(S, n_)
                s_.wait(V, n_)
        dBP.wait(TE, 16)
        aw_load(wpt, 0)
        aw_load(wpt, 1)
        lanes0 = chain(0)
        wproj_mms(0, lanes0)
        aw_load(wpt, 0)
        aw_load(wpt, 1)
        lanes1 = chain(1)
        if DEBUG_ATTN and c == 0:
            cV.wait(SY)
            cS.wait(SY)
            for i_, ap_ in enumerate([pool[:, 4096:8192], pool[:, 12296:16392],
                                      pool[:, 0:4096], pool[:, 16392:20496],
                                      pool[:, 8192:12296], pool[:, 20496:24592]]):
                n_ = ap_.shape[1]
                dDB.inc(SY.dma_start(out=bass.AP(dbg, i_ * 128 * 4104,
                                                [[4104, 128], [1, n_]]),
                                    in_=ap_), 16)
            dDB.inc(SY.dma_start(out=bass.AP(dbgq, 0, [[4104, 128], [1, 4104]]),
                                in_=poolb[:, 4096:8200]), 16)
            dDB.inc(SY.dma_start(out=bass.AP(dbgq, 128 * 4104,
                                            [[4104, 128], [1, 4096]]),
                                in_=poolb[:, 8200:12296]), 16)
            dDB.wait(V)
            dDB.wait(S)
            dDB.wait(TE)
        wproj_evac(0)
        ln_stats_part(xb, (0, 1))
        wproj_mms(1, lanes1)
        wproj_evac(1)

    emit_ffn_pre(0, xsb[0], True, 0)
    emit_tern_stream()
    tern_done["gu"] = [(x, x.n) for x in dTG + dTU]
    tern_done["d"] = [(x, x.n) for x in dTD]
    tern_done["ti"] = [(x, x.n) for x in dTI]

    for c in range(NCH):
        xb = xsb[c % 2]
        if c + 1 < NCH:
            if c >= 1:
                dO.wait(SY, 16 * c)
            dX[(c + 1) % 2].inc(SY.dma_start(
                out=xsb[(c + 1) % 2][:, :, :],
                in_=x_hbm.ap()[(c + 1) * T:(c + 2) * T, :]
                .rearrange("(a p) d -> p a d", p=128)), 16)
        emit_ffn_rest(c, xb, c % 2,
                      attn_next=(lambda: attn_pre_tts(xb, (0, 1))))
        emit_attn(c, xb, pre01_done=True)
        emit_ffn_pre(c, xb, False, c % 2, skip01=True)
        pn = None
        if c + 1 < NCH:
            pn = (lambda cc: (lambda: emit_ffn_pre(cc, xsb[cc % 2], True,
                                                   cc % 2)))(c + 1)
        emit_ffn_rest(c, xb, c % 2, pre_next=pn)
        cV.wait(SY)
        cS.wait(SY)
        dO.inc(SY.dma_start(
            out=y_hbm.ap()[c * T:(c + 1) * T, :].rearrange("(a p) d -> p a d",
                                                           p=128),
            in_=xb[:, :, :]), 16)
    SY.wait_ge(dO.h, 16 * NCH)
    st.close()
    return nc


def kernel(**inp):
    def f32c(a):
        return np.ascontiguousarray(np.asarray(a, dtype=np.float32))

    x = f32c(inp["L0_attn_x"])
    B, L, _ = x.shape
    wT = {
        "wq1t": f32c(np.asarray(inp["L0_attn_Wq1"]).T),
        "wk1t": f32c(np.asarray(inp["L0_attn_Wk1"]).T),
        "wk2t": f32c(np.asarray(inp["L0_attn_Wk2"]).T),
        "wpt": f32c(np.asarray(inp["L0_attn_Wproj"]).T),
        "wgt": f32c(np.asarray(inp["L0_ffn_w_gate"]).T),
        "wut": f32c(np.asarray(inp["L0_ffn_w_up"]).T),
        "wdt": f32c(np.asarray(inp["L0_ffn_w_down"]).T),
        "bq1": f32c(inp["L0_attn_bq1"]),
        "bk1": f32c(inp["L0_attn_bk1"]),
        "bk2": f32c(inp["L0_attn_bk2"]),
        "bpr": f32c(inp["L0_attn_bproj"]),
    }
    nc = build_nc(L=L, T=512)
    in_maps = [dict(x=x[c], **wT) for c in range(B)]
    res = run_bass_kernel_spmd(nc, in_maps, core_ids=list(range(B)))
    return np.stack([np.asarray(res.results[c]["y"], dtype=np.float32)
                     for c in range(B)], axis=0)


# version 18
# speedup vs baseline: 1.0537x; 1.0243x over previous
import numpy as np
from contextlib import ExitStack

import concourse.bass as bass
import concourse.mybir as mybir
from concourse.bass_utils import run_bass_kernel_spmd

F32 = mybir.dt.float32
BF16 = mybir.dt.bfloat16
AF = mybir.ActivationFunctionType
ALU = mybir.AluOpType
AXX = mybir.AxisListType.X

MAGIC = 12582912.0
K2SCALE = float(np.float32(0.02)) / 32.0
EPS_LN = 1e-5
EPS_KC = 1e-6
EPS_Q = 1e-5
D = 1024
INNER = 4096
DC = D // 128
DEBUG_ATTN = False


class Sem:

    def __init__(self, st, nc, name):
        self.h = st.enter_context(nc.semaphore(name))
        self.n = 0

    def inc(self, instr, k=1):
        instr.then_inc(self.h, k)
        self.n += k
        return instr

    def wait(self, eng, n=None):
        eng.wait_ge(self.h, self.n if n is None else n)


def build_nc(L=4096, T=512):
    NCH = L // T
    TT = T // 128
    assert T % 128 == 0 and L % T == 0 and TT == 4

    nc = bass.Bass("TRN2", target_bir_lowering=False)
    x_hbm = nc.dram_tensor("x", [L, D], F32, kind="ExternalInput")
    y_hbm = nc.dram_tensor("y", [L, D], F32, kind="ExternalOutput")
    wq1t = nc.dram_tensor("wq1t", [D, D], F32, kind="ExternalInput")
    wk1t = nc.dram_tensor("wk1t", [D, D], F32, kind="ExternalInput")
    wk2t = nc.dram_tensor("wk2t", [D, D], F32, kind="ExternalInput")
    wpt = nc.dram_tensor("wpt", [D, D], F32, kind="ExternalInput")
    wgt = nc.dram_tensor("wgt", [D, INNER], F32, kind="ExternalInput")
    wut = nc.dram_tensor("wut", [D, INNER], F32, kind="ExternalInput")
    wdt = nc.dram_tensor("wdt", [INNER, D], F32, kind="ExternalInput")
    bq1 = nc.dram_tensor("bq1", [D], F32, kind="ExternalInput")
    bk1 = nc.dram_tensor("bk1", [D], F32, kind="ExternalInput")
    bk2 = nc.dram_tensor("bk2", [D], F32, kind="ExternalInput")
    bpr = nc.dram_tensor("bpr", [D], F32, kind="ExternalInput")
    sc_g = nc.dram_tensor("sc_g", [8, 128, 8, 512], BF16, kind="Internal")
    sc_u = nc.dram_tensor("sc_u", [8, 128, 8, 512], BF16, kind="Internal")
    sc_d = nc.dram_tensor("sc_d", [8, 128, 4, 1024], BF16, kind="Internal")
    if DEBUG_ATTN:
        dbg = nc.dram_tensor("dbg", [6, 128, 4104], F32, kind="ExternalOutput")
        dbgq = nc.dram_tensor("dbgq", [2, 128, 4104], BF16, kind="ExternalOutput")
        dbg2 = nc.dram_tensor("dbg2", [128, 4104], F32, kind="ExternalOutput")

    st = ExitStack()
    E = st.enter_context
    xsb = [E(nc.sbuf_tensor(f"xsb{i}", [128, TT, D], F32)) for i in range(2)]
    pool = E(nc.sbuf_tensor("pool", [128, 25616], F32))
    r3 = lambda ap, a: ap.rearrange("p (a b) -> p a b", a=a)
    VT = r3(pool[:, 0:4096], 8)
    KB = r3(pool[:, 4096:8192], 8)
    QB = r3(pool[:, 8192:12296], 8)
    KV = r3(pool[:, 12296:16392], 8)
    OAR = r3(pool[:, 16392:20496], 8)
    K2B = r3(pool[:, 20496:24592], 8)
    silu2 = pool[:, 24592:25616]
    hbuf = r3(pool[:, 0:TT * 4096], TT)
    qstage = pool[:, 0:TT * 1024]
    tpiece = pool[:, 20496:24592]
    poolb = E(nc.sbuf_tensor("poolb", [128, 20488], BF16))
    aq = poolb[:, 0:4096].rearrange("p (a b c) -> p a b c", a=TT, b=DC)
    hq = poolb[:, 4096:20480].rearrange("p (a b c) -> p a b c", a=TT, b=32)
    xn2 = r3(poolb[:, 0:4096], TT)
    xn2T = r3(poolb[:, 4096:8200], 8)
    OB_ = r3(poolb[:, 8200:12296], 8)
    tstage = poolb[:, 12296:16392]
    awb = poolb[:, 12296:20488]
    sarena = E(nc.sbuf_tensor("sarena", [128, 12288], BF16))
    gsl = sarena[:, 0:4096]
    usl = sarena[:, 4096:8192]
    dsl = sarena[:, 8192:12288]
    ident = E(nc.sbuf_tensor("ident", [128, 128], F32))
    iota_p = pool[:, 0:128]
    iota_f = pool[:, 128:256]
    onesc = E(nc.sbuf_tensor("onesc", [128, 1], F32))
    onesr32 = E(nc.sbuf_tensor("onesr32", [1, 128], F32))
    onesrb = E(nc.sbuf_tensor("onesrb", [1, 128], BF16))
    bprow = E(nc.sbuf_tensor("bprow", [1, D], BF16))
    bsb = E(nc.sbuf_tensor("bsb", [128, 3 * DC], F32))
    smal = E(nc.sbuf_tensor("smal", [128, 288], F32))
    mean1 = E(nc.sbuf_tensor("mean1", [1, 8], F32))
    stats = smal[:, 0:48]
    mv = smal[:, 48:56]
    rstd = smal[:, 56:60]
    nmr = smal[:, 60:64]
    hi = smal[:, 64:68]
    lo = smal[:, 68:72]
    t1 = smal[:, 72:76]
    t2 = smal[:, 76:80]
    cmax = smal[:, 80:84]
    s1 = smal[:, 84:88]
    Aq = smal[:, 88:92]
    Bq = smal[:, 92:96]
    cg = smal[:, 96:100]
    cu = smal[:, 100:104]
    hmax = smal[:, 104:108]
    s2v = smal[:, 108:112]
    cdv = smal[:, 112:116]
    kc_carry = smal[:, 116:124]
    kv_carry = smal[:, 124:132]
    s2_carry = smal[:, 132:140]
    swb = smal[:, 140:143]
    asum = smal[:, 144:160]
    oarc = smal[:, 160:168]
    qc = smal[:, 168:176]
    xnc = smal[:, 176:184]
    preS = [smal[:, 192:224], smal[:, 224:256]]
    magic_ap = smal[:, 184:185]
    cmb = smal[:, 188:191]
    epsln_ap = smal[:, 185:186]
    epskc_ap = smal[:, 186:187]
    nmagic_ap = smal[:, 187:188]
    PB = E(nc.psum_tensor("PB", [128, 4096], F32))
    pb = [PB[:, k * 512:(k + 1) * 512] for k in range(8)]

    dX = [Sem(st, nc, "dX0"), Sem(st, nc, "dX1")]
    dO = Sem(st, nc, "dO")
    dGU = [Sem(st, nc, "dGU0"), Sem(st, nc, "dGU1")]
    dDN = [Sem(st, nc, "dDN0"), Sem(st, nc, "dDN1")]
    dTI = [Sem(st, nc, f"dTI{i}") for i in range(4)]
    dTG = [Sem(st, nc, f"dTG{i}") for i in range(4)]
    dTU = [Sem(st, nc, f"dTU{i}") for i in range(4)]
    dTD = [Sem(st, nc, f"dTD{i}") for i in range(4)]
    dAW = [Sem(st, nc, "dAW0"), Sem(st, nc, "dAW1"),
           Sem(st, nc, "dAW2"), Sem(st, nc, "dAW3")]
    dBP = Sem(st, nc, "dBP")
    dB = Sem(st, nc, "dB")
    dDB = Sem(st, nc, "dDB")
    cV = Sem(st, nc, "cV")
    cS = Sem(st, nc, "cS")
    cT = Sem(st, nc, "cT")
    cG = Sem(st, nc, "cG")

    V, S, TE, SY, G = nc.vector, nc.scalar, nc.tensor, nc.sync, nc.gpsimd

    G.iota(iota_p[:, :], [[0, 128]], channel_multiplier=1,
           allow_small_or_imprecise_dtypes=True)
    cG.inc(G.iota(iota_f[:, :], [[1, 128]], channel_multiplier=0,
                  allow_small_or_imprecise_dtypes=True))
    cG.wait(V)
    V.tensor_tensor(out=ident[:], in0=iota_p[:], in1=iota_f[:], op=ALU.is_equal)
    V.memset(onesc[:], 1.0)
    V.memset(onesr32[:], 1.0)
    V.memset(onesrb[:], 1.0)
    V.memset(kc_carry, 0.0)
    V.memset(kv_carry, 0.0)
    V.memset(s2_carry, 0.0)
    V.memset(oarc, 0.0)
    V.memset(qc, 0.0)
    V.memset(xnc, 0.0)
    V.memset(magic_ap, MAGIC)
    V.memset(epsln_ap, EPS_LN)
    V.memset(epskc_ap, EPS_KC)
    cV.inc(V.memset(nmagic_ap, -MAGIC))
    dB.inc(SY.dma_start(out=bsb[:, 0:DC],
                        in_=bq1.ap().rearrange("(j p) -> p j", p=128),
                        allow_slow_non_contiguous=True), 16)
    dB.inc(SY.dma_start(out=bsb[:, DC:2 * DC],
                        in_=bk1.ap().rearrange("(j p) -> p j", p=128),
                        allow_slow_non_contiguous=True), 16)
    dB.inc(SY.dma_start(out=bsb[:, 2 * DC:3 * DC],
                        in_=bk2.ap().rearrange("(j p) -> p j", p=128),
                        allow_slow_non_contiguous=True), 16)
    dBP.inc(G.dma_start(out=bprow[:, :], in_=bpr.ap()[None, :]), 16)
    b_q1 = bsb[:, 0:DC]
    b_k1 = bsb[:, DC:2 * DC]
    b_k2s = bsb[:, 2 * DC:3 * DC]
    dB.wait(V)
    cV.inc(V.tensor_scalar(out=b_k2s, in0=b_k2s, scalar1=K2SCALE, scalar2=None,
                           op0=ALU.mult))
    dX[0].inc(SY.dma_start(out=xsb[0][:, :, :],
                           in_=x_hbm.ap()[0:T, :]
                           .rearrange("(a p) d -> p a d", p=128)), 16)
    tern_done = {}

    ttr = smal[:, 191:192]
    tpc = [tpiece[:, 0:2048], tpiece[:, 2048:4096],
           pool[:, 16392:18440], pool[:, 18440:20488]]
    tst = [poolb[:, 12296:14344], poolb[:, 14344:16392],
           poolb[:, 8200:10248], poolb[:, 10248:12296]]
    tw_state = {"idx": 0, "slotV": [None] * 4, "stgV": [None] * 4}

    def tern_dma_in(src_ap, pf):
        i = tw_state["idx"]
        tw_state["idx"] += 1
        bsel = i % 4
        buf = tpc[bsel][:, 0:pf]
        if tw_state["slotV"][bsel] is not None:
            sem_, n_ = tw_state["slotV"][bsel]
            SY.wait_ge(sem_.h, n_)
        else:
            cV.wait(SY)
        dTI[bsel].inc(SY.dma_start(out=buf, in_=src_ap), 16)
        return bsel, buf

    def mean_piece(src_ap, pf, acc_slot):
        bsel, buf = tern_dma_in(src_ap, pf)
        dTI[bsel].wait(S)
        cS.inc(S.activation(out=buf, in_=buf, func=AF.Abs,
                            accum_out=acc_slot))
        tw_state["slotV"][bsel] = (cS, cS.n)

    def mean_finalize(sw_slot, inv_n, cm_slot, cmf):
        cS.wait(V)
        cV.inc(V.reduce_sum(out=ttr, in_=asum[:, :], axis=AXX))
        cV.wait(TE)
        cS.wait(TE)
        cT.inc(TE.matmul(out=pb[0][0:1, 0:1], lhsT=onesc[:, :], rhs=ttr,
                         start=True, stop=True))
        cT.wait(V)
        cV.inc(V.tensor_scalar(out=mean1[:, 0:1], in0=pb[0][0:1, 0:1],
                               scalar1=inv_n, scalar2=1e-5,
                               op0=ALU.mult, op1=ALU.max))
        cV.wait(V)
        V.reciprocal(out=mean1[:, 2:3], in_=mean1[:, 0:1])
        cV.inc(V.tensor_scalar(out=mean1[:, 3:4], in0=mean1[:, 0:1],
                               scalar1=cmf, scalar2=None, op0=ALU.mult))
        cV.wait(TE)
        cS.wait(TE)
        TE.matmul(out=pb[1][:, 0:1], lhsT=onesr32[0:1, :],
                  rhs=mean1[0:1, 2:3], start=True, stop=True)
        cT.inc(TE.matmul(out=pb[2][:, 0:1], lhsT=onesr32[0:1, :],
                         rhs=mean1[0:1, 3:4], start=True, stop=True))
        cT.wait(V)
        V.tensor_copy(out=sw_slot, in_=pb[1][:, 0:1])
        cV.inc(V.tensor_copy(out=cm_slot, in_=pb[2][:, 0:1]))

    def tern_mean(w_hbm, n_rows, n_free, sw_slot, inv_n, cm_slot, cmf):
        kchunks = n_rows // 128
        cV.wait(S)
        idx = 0
        if n_free >= 2048:
            for kc in range(kchunks):
                for j in range(n_free // 2048):
                    src = bass.AP(w_hbm, kc * 128 * n_free + j * 2048,
                                  [[n_free, 128], [1, 2048]])
                    mean_piece(src, 2048, asum[:, idx:idx + 1])
                    idx += 1
        else:
            pr = 2048 // n_free
            for kc in range(0, kchunks, pr):
                src = bass.AP(w_hbm, kc * 128 * n_free,
                              [[n_free, 128], [128 * n_free, pr], [1, n_free]])
                mean_piece(src, n_free * pr, asum[:, idx:idx + 1])
                idx += 1
        mean_finalize(sw_slot, inv_n, cm_slot, cmf)

    def tern_piece(src_ap, dst_ap, sw_slot, pf, osems, sw_wait=False):
        bsel, buf = tern_dma_in(src_ap, pf)
        sbf = tst[bsel][:, 0:pf]
        dTI[bsel].wait(V)
        cV.wait(V)
        cV.inc(V.tensor_scalar(out=buf, in0=buf, scalar1=sw_slot, scalar2=MAGIC,
                               op0=ALU.mult, op1=ALU.add))
        cV.wait(V)
        cV.inc(V.tensor_scalar(out=buf, in0=buf, scalar1=MAGIC, scalar2=1.0,
                               op0=ALU.subtract, op1=ALU.min))
        cV.wait(V)
        prev = tw_state["stgV"][bsel]
        if prev is not None:
            prev[0].wait(V, prev[1])
        cV.inc(V.tensor_scalar(out=sbf, in0=buf, scalar1=-1.0, scalar2=None,
                               op0=ALU.max))
        tw_state["slotV"][bsel] = (cV, cV.n)
        cV.wait(G)
        osem = osems[bsel]
        osem.inc(G.dma_start(out=dst_ap, in_=sbf), 16)
        tw_state["stgV"][bsel] = (osem, osem.n)

    def tern_write_gu(w_hbm, scratch, sw_slot, osems):
        for ob in range(8):
            for icg in range(2):
                src = bass.AP(w_hbm, icg * 4 * 128 * INNER + ob * 512,
                              [[INNER, 128], [128 * INNER, 4], [1, 512]])
                dst = bass.AP(scratch, ob * (128 * 8 * 512) + icg * 4 * 512,
                              [[8 * 512, 128], [512, 4], [1, 512]])
                tern_piece(src, dst, sw_slot, 2048, osems,
                           sw_wait=(ob == 0 and icg == 0))

    def tern_write_d(w_hbm, scratch, sw_slot, osems):
        for icg in range(8):
            for h2 in range(2):
                src = bass.AP(w_hbm, (icg * 4 + h2 * 2) * 128 * D,
                              [[D, 128], [128 * D, 2], [1, D]])
                dst = bass.AP(scratch, icg * (128 * 4 * 1024) + h2 * 2 * 1024,
                              [[4 * 1024, 128], [1024, 2], [1, 1024]])
                tern_piece(src, dst, sw_slot, 2048, osems,
                           sw_wait=(icg == 0 and h2 == 0))

    def emit_tern_stream():
        tern_mean(wgt, D, INNER, swb[:, 0:1], 1.0 / (D * INNER), cmb[:, 0:1],
                  1.0 / 127.0)
        tern_write_gu(wgt, sc_g, swb[:, 0:1], dTG)
        tern_mean(wut, D, INNER, swb[:, 1:2], 1.0 / (D * INNER), cmb[:, 1:2],
                  1.0 / 127.0)
        tern_write_gu(wut, sc_u, swb[:, 1:2], dTU)
        tern_mean(wdt, INNER, D, swb[:, 2:3], 1.0 / (D * INNER), cmb[:, 2:3],
                  1.0 / 254.0)
        tern_write_d(wdt, sc_d, swb[:, 2:3], dTD)

    def ln_stats_part(xb, tts):
        cV.wait(V)
        last = None
        for tt in tts:
            V.bn_stats(out=stats[:, tt * 12:tt * 12 + 6], in_=xb[:, tt, 0:512])
            V.bn_stats(out=stats[:, tt * 12 + 6:tt * 12 + 12],
                       in_=xb[:, tt, 512:1024])
            V.reduce_max(out=hi[:, tt:tt + 1], in_=xb[:, tt, :], axis=AXX)
            last = V.tensor_reduce(out=lo[:, tt:tt + 1], in_=xb[:, tt, :],
                                   axis=AXX, op=ALU.min)
        cV.inc(last)

    def ln_stats(xb, skip01=False):
        ln_stats_part(xb, (2, 3) if skip01 else (0, 1, 2, 3))
        cV.wait(V)
        for tt in range(TT - 1):
            V.bn_aggr(out=mv[:, tt * 2:tt * 2 + 2],
                      in_=stats[:, tt * 12:tt * 12 + 12])
        tt = TT - 1
        cV.inc(V.bn_aggr(out=mv[:, tt * 2:tt * 2 + 2],
                         in_=stats[:, tt * 12:tt * 12 + 12]))

    def ln_finalize():
        cV.wait(S)
        cS.inc(S.activation(out=rstd, in_=mv[:, 1:8:2], func=AF.Ln, bias=epsln_ap))
        cS.wait(S)
        cS.inc(S.activation(out=rstd, in_=rstd, func=AF.Exp, scale=-0.5))

    def quant_scales():
        cS.wait(V)
        V.tensor_tensor(out=t1, in0=hi, in1=mv[:, 0:8:2], op=ALU.subtract)
        cV.inc(V.tensor_tensor(out=t2, in0=mv[:, 0:8:2], in1=lo, op=ALU.subtract))
        cV.wait(V)
        cV.inc(V.tensor_tensor(out=cmax, in0=t1, in1=t2, op=ALU.max))
        cV.wait(V)
        cV.inc(V.tensor_tensor(out=t1, in0=rstd, in1=cmax, op=ALU.mult))
        cV.wait(V)
        cV.inc(V.tensor_scalar(out=t1, in0=t1, scalar1=EPS_Q, scalar2=None,
                               op0=ALU.max))
        cV.wait(V)
        cV.inc(V.reciprocal(out=t2, in_=t1))
        cV.wait(V)
        cV.inc(V.scalar_tensor_tensor(out=Aq, in0=t2, scalar=127.0, in1=rstd,
                                      op0=ALU.mult, op1=ALU.mult))
        cV.wait(V)
        cV.inc(V.tensor_tensor(out=Bq, in0=mv[:, 0:8:2], in1=Aq, op=ALU.mult))
        cV.wait(V)
        cV.inc(V.tensor_scalar(out=Bq, in0=Bq, scalar1=-1.0, scalar2=MAGIC,
                               op0=ALU.mult, op1=ALU.add))

    def transpose_quads(quads, evac="copy", eng="alt", saved=None):
        v0, s0 = saved if saved is not None else (cV.n, cS.n)
        savedV = [v0] * 4
        savedS = [s0] * 4
        for qi, (srcs, dst) in enumerate(quads):
            b = qi % 4
            TE.wait_ge(cV.h, savedV[b])
            TE.wait_ge(cS.h, savedS[b])
            for j, s_ in enumerate(srcs[:-1]):
                TE.transpose(out=PB[:, b * 1024 + j * 128:
                                    b * 1024 + (j + 1) * 128],
                             in_=s_, identity=ident[:, :])
            cT.inc(TE.transpose(
                out=PB[:, b * 1024 + (len(srcs) - 1) * 128:
                       b * 1024 + len(srcs) * 128],
                in_=srcs[-1], identity=ident[:, :]))
            pt = PB[:, b * 1024:b * 1024 + len(srcs) * 128]
            if eng == "V" or (eng == "alt" and qi % 2 == 0):
                cT.wait(V)
                if evac == "copy":
                    cV.inc(V.tensor_copy(out=dst, in_=pt))
                else:
                    cV.inc(V.tensor_scalar(out=dst, in0=pt, scalar1=MAGIC,
                                           scalar2=None, op0=ALU.subtract))
                savedV[b] = cV.n
            else:
                cT.wait(S)
                if evac == "copy":
                    cS.inc(S.copy(out=dst, in_=pt))
                else:
                    cS.inc(S.add(out=dst, in_=pt, add=nmagic_ap))
                savedS[b] = cS.n

    def xstage(tt):
        hs, ti = tt // 2, tt % 2
        return VT[:, :, hs * 256 + ti * 128:hs * 256 + ti * 128 + 128]

    def attn_pre_tts(xb, tts):
        ln_stats_part(xb, tts)
        cV.wait(V)
        last = None
        for tt in tts:
            last = V.bn_aggr(out=mv[:, tt * 2:tt * 2 + 2],
                             in_=stats[:, tt * 12:tt * 12 + 12])
        cV.inc(last)
        t0 = tts[0]
        cV.wait(S)
        cS.inc(S.activation(out=rstd[:, t0:t0 + 2],
                            in_=mv[:, 2 * t0 + 1:2 * t0 + 4:2],
                            func=AF.Ln, bias=epsln_ap))
        cS.wait(S)
        cS.inc(S.activation(out=rstd[:, t0:t0 + 2], in_=rstd[:, t0:t0 + 2],
                            func=AF.Exp, scale=-0.5))
        cS.wait(V)
        cV.inc(V.scalar_tensor_tensor(out=nmr[:, t0:t0 + 2],
                                      in0=mv[:, 2 * t0:2 * t0 + 4:2],
                                      scalar=-1.0, in1=rstd[:, t0:t0 + 2],
                                      op0=ALU.mult, op1=ALU.mult))
        cV.wait(S)
        sl_ = None
        for tt in tts:
            sl_ = S.activation(out=xstage(tt), in_=xb[:, tt, :],
                               func=AF.Identity, scale=rstd[:, tt:tt + 1],
                               bias=nmr[:, tt:tt + 1])
        cS.inc(sl_)

    def emit_ffn_pre(c, xb, first_of_chunk, p, skip01=False):
        ps = preS[p]
        pmv, prstd = ps[:, 0:8], ps[:, 8:12]
        pAq, pBq = ps[:, 12:16], ps[:, 16:20]
        pcg, pcu = ps[:, 20:24], ps[:, 24:28]
        pt1 = ps[:, 28:32]
        if first_of_chunk:
            dX[c % 2].wait(V)
        ln_stats(xb, skip01=skip01)
        cV.wait(S)
        cS.inc(S.activation(out=prstd, in_=mv[:, 1:8:2], func=AF.Ln,
                            bias=epsln_ap))
        cS.wait(S)
        cS.inc(S.activation(out=prstd, in_=prstd, func=AF.Exp, scale=-0.5))
        cS.wait(V)
        cV.inc(V.tensor_copy(out=pmv, in_=mv))
        cV.wait(V)
        V.tensor_tensor(out=t1, in0=hi, in1=mv[:, 0:8:2], op=ALU.subtract)
        cV.inc(V.tensor_tensor(out=t2, in0=mv[:, 0:8:2], in1=lo,
                               op=ALU.subtract))
        cV.wait(V)
        cV.inc(V.tensor_tensor(out=cmax, in0=t1, in1=t2, op=ALU.max))
        cV.wait(V)
        cV.inc(V.tensor_tensor(out=pt1, in0=prstd, in1=cmax, op=ALU.mult))
        cV.wait(V)
        cV.inc(V.tensor_scalar(out=pt1, in0=pt1, scalar1=EPS_Q, scalar2=None,
                               op0=ALU.max))
        cV.wait(V)
        cV.inc(V.reciprocal(out=t2, in_=pt1))
        cV.wait(V)
        cV.inc(V.scalar_tensor_tensor(out=pAq, in0=t2, scalar=127.0, in1=prstd,
                                      op0=ALU.mult, op1=ALU.mult))
        cV.wait(V)
        cV.inc(V.tensor_tensor(out=pBq, in0=pmv[:, 0:8:2], in1=pAq,
                               op=ALU.mult))
        cV.wait(V)
        cV.inc(V.tensor_scalar(out=pBq, in0=pBq, scalar1=-1.0, scalar2=MAGIC,
                               op0=ALU.mult, op1=ALU.add))
        cV.wait(S)
        sl_ = None
        for tt in range(TT):
            sl_ = S.activation(out=qstage[:, tt * 1024:(tt + 1) * 1024],
                               in_=xb[:, tt, :], func=AF.Identity,
                               scale=pAq[:, tt:tt + 1], bias=pBq[:, tt:tt + 1])
        cS.inc(sl_)
        cS.wait(TE)
        transpose_quads(
            [([qstage[:, tt * 1024 + ic * 128:tt * 1024 + (ic + 1) * 128]
               for ic in range(DC)],
              aq[:, tt, :, :]) for tt in range(TT)], evac="submagic")

    def emit_ffn_rest(c, xb, p, pre_next=None, attn_next=None):
        cg = preS[p][:, 20:24]
        cu = preS[p][:, 24:28]
        pt1 = preS[p][:, 28:32]
        cV.wait(V)
        V.tensor_scalar(out=cg, in0=pt1, scalar1=cmb[:, 0:1], scalar2=None,
                        op0=ALU.mult)
        cV.inc(V.tensor_scalar(out=cu, in0=pt1, scalar1=cmb[:, 1:2],
                               scalar2=None, op0=ALU.mult))
        gu_arena_cT = [cT.n, cT.n]
        evac_saved = [(cV.n, cS.n), (cV.n, cS.n)]
        slot_cV = [cV.n, cV.n]

        def gu_load(k):
            ob_, icg_ = divmod(k, 2)
            bs = k % 2
            S.wait_ge(cT.h, gu_arena_cT[bs])
            src_g = bass.AP(sc_g, ob_ * (128 * 8 * 512) + icg_ * 4 * 512,
                            [[8 * 512, 128], [512, 4], [1, 512]])
            src_u = bass.AP(sc_u, ob_ * (128 * 8 * 512) + icg_ * 4 * 512,
                            [[8 * 512, 128], [512, 4], [1, 512]])
            dGU[bs].inc(S.dma_start(
                out=gsl[:, bs * 2048:(bs + 1) * 2048], in_=src_g), 16)
            dGU[bs].inc(S.dma_start(
                out=usl[:, bs * 2048:(bs + 1) * 2048], in_=src_u), 16)

        if c == 0:
            for s_, n_ in tern_done["gu"]:
                s_.wait(S, n_)
        gu_load(0)
        gu_load(1)

        hpart = smal[:, 256:288]

        def gu_evac(ob_, ttg):
            cT.wait(S)
            cT.wait(V)
            for tt in (ttg * 2, ttg * 2 + 1):
                slot = tt % 2
                sl = silu2[:, slot * 512:slot * 512 + 512]
                S.wait_ge(cV.h, slot_cV[slot])
                cS.inc(S.activation(out=sl, in_=pb[ttg * 4 + (tt % 2)][:, :],
                                    func=AF.Silu, scale=cg[:, tt:tt + 1]))
                cS.wait(V)
                cV.inc(V.scalar_tensor_tensor(
                    out=hbuf[:, tt, ob_ * 512:(ob_ + 1) * 512],
                    in0=pb[ttg * 4 + 2 + (tt % 2)][:, :],
                    scalar=cu[:, tt:tt + 1], in1=sl,
                    op0=ALU.mult, op1=ALU.mult))
                cV.wait(V)
                cV.inc(V.reduce_max(out=hpart[:, tt * 8 + ob_:tt * 8 + ob_ + 1],
                                    in_=hbuf[:, tt, ob_ * 512:(ob_ + 1) * 512],
                                    axis=AXX, apply_absolute_value=True))
                slot_cV[slot] = cV.n
            evac_saved[ttg] = (cV.n, cS.n)

        first_mm = True
        for ob in range(8):
            for icg in range(2):
                k = ob * 2 + icg
                bsel = k % 2
                dGU[bsel].wait(TE)
                if first_mm:
                    cV.wait(TE)
                    cS.wait(TE)
                    first_mm = False
                for ttg in range(2):
                    if icg == 0:
                        TE.wait_ge(cV.h, evac_saved[ttg][0])
                        TE.wait_ge(cS.h, evac_saved[ttg][1])
                    tl_ = None
                    for ic4 in range(4):
                        ic = icg * 4 + ic4
                        for tt in (ttg * 2, ttg * 2 + 1):
                            TE.matmul(out=pb[ttg * 4 + (tt % 2)][:, :],
                                      lhsT=aq[:, tt, ic, :],
                                      rhs=gsl[:, bsel * 2048 + ic4 * 512:
                                              bsel * 2048 + (ic4 + 1) * 512],
                                      start=(ic == 0), stop=(ic == 7))
                            tl_ = TE.matmul(
                                out=pb[ttg * 4 + 2 + (tt % 2)][:, :],
                                lhsT=aq[:, tt, ic, :],
                                rhs=usl[:, bsel * 2048 + ic4 * 512:
                                        bsel * 2048 + (ic4 + 1) * 512],
                                start=(ic == 0), stop=(ic == 7))
                    cT.inc(tl_)
                    if icg == 1:
                        gu_evac(ob, ttg)
                gu_arena_cT[bsel] = cT.n
                if k + 2 < 16:
                    gu_load(k + 2)
        cV.wait(V)
        if c == 0:
            for s_, n_ in tern_done["d"]:
                s_.wait(V, n_)
                s_.wait(S, n_)
        hq_saved = (cV.n, cS.n)
        p1_cS = []
        for tt in range(TT):
            cV.inc(V.reduce_max(out=hmax[:, tt:tt + 1],
                                in_=hpart[:, tt * 8:tt * 8 + 8], axis=AXX))
            cV.wait(V)
            cV.inc(V.tensor_scalar(out=hmax[:, tt:tt + 1],
                                   in0=hmax[:, tt:tt + 1], scalar1=EPS_Q,
                                   scalar2=None, op0=ALU.max))
            cV.wait(V)
            V.reciprocal(out=t1[:, tt:tt + 1], in_=hmax[:, tt:tt + 1])
            cV.inc(V.tensor_scalar(out=cdv[:, tt:tt + 1],
                                   in0=hmax[:, tt:tt + 1],
                                   scalar1=cmb[:, 2:3], scalar2=None,
                                   op0=ALU.mult))
            cV.wait(V)
            cV.inc(V.tensor_scalar(out=s2v[:, tt:tt + 1], in0=t1[:, tt:tt + 1],
                                   scalar1=127.0, scalar2=None, op0=ALU.mult))
            cV.wait(S)
            cS.inc(S.activation(out=hbuf[:, tt, :], in_=hbuf[:, tt, :],
                                func=AF.Identity, scale=s2v[:, tt:tt + 1],
                                bias=magic_ap))
            p1_cS.append(cS.n)
        for tt in range(TT):
            TE.wait_ge(cS.h, p1_cS[tt])
            transpose_quads(
                [([hbuf[:, tt, ic * 128:(ic + 1) * 128]
                   for ic in range(q * 8, q * 8 + 8)],
                  hq[:, tt, q * 8:q * 8 + 8, :]) for q in range(4)],
                evac="submagic", saved=hq_saved)
            hq_saved = None
        if pre_next is not None:
            pre_next()
        dn_arena_cT = [cT.n, cT.n]

        def dn_load(ih_):
            bs = ih_ % 2
            S.wait_ge(cT.h, dn_arena_cT[bs])
            src_d = bass.AP(sc_d,
                            (ih_ // 2) * (128 * 4 * 1024) + (ih_ % 2) * 2048,
                            [[4 * 1024, 128], [1024, 2], [1, 1024]])
            dDN[bs].inc(S.dma_start(out=dsl[:, bs * 2048:(bs + 1) * 2048],
                                    in_=src_d), 16)

        if c == 0:
            for s_, n_ in tern_done["d"]:
                s_.wait(S, n_)
        dn_load(0)
        dn_load(1)
        for ih in range(16):
            bsel = ih % 2
            dDN[bsel].wait(TE)
            if ih == 0:
                cV.wait(TE)
                cS.wait(TE)
            tl_ = None
            for ic2 in range(2):
                ic = ih * 2 + ic2
                for tt in range(TT):
                    for obk in range(2):
                        tl_ = TE.matmul(
                            out=pb[tt * 2 + obk][:, :], lhsT=hq[:, tt, ic, :],
                            rhs=dsl[:, bsel * 2048 + ic2 * 1024 + obk * 512:
                                    bsel * 2048 + ic2 * 1024 + (obk + 1) * 512],
                            start=(ic == 0), stop=(ic == 31))
            cT.inc(tl_)
            dn_arena_cT[bsel] = cT.n
            if ih + 2 < 16:
                dn_load(ih + 2)
        cT.wait(V)
        vl_ = None
        for tt in (0, 1):
            for obk in range(2):
                vl_ = V.scalar_tensor_tensor(
                    out=xb[:, tt, obk * 512:(obk + 1) * 512],
                    in0=pb[tt * 2 + obk][:, :], scalar=cdv[:, tt:tt + 1],
                    in1=xb[:, tt, obk * 512:(obk + 1) * 512],
                    op0=ALU.mult, op1=ALU.add)
        cV.inc(vl_)
        if attn_next is not None:
            attn_next()
        cV.wait(V)
        vl_ = None
        for tt in (2, 3):
            for obk in range(2):
                vl_ = V.scalar_tensor_tensor(
                    out=xb[:, tt, obk * 512:(obk + 1) * 512],
                    in0=pb[tt * 2 + obk][:, :], scalar=cdv[:, tt:tt + 1],
                    in1=xb[:, tt, obk * 512:(obk + 1) * 512],
                    op0=ALU.mult, op1=ALU.add)
        cV.inc(vl_)

    kc_mid = smal[:, 144:152]
    kv_mid = smal[:, 152:160]
    s2_mid = smal[:, 40:48]

    def emit_attn(c, xb, pre01_done=False):
        HT = T // 2
        if not pre01_done:
            attn_pre_tts(xb, (0, 1))
        cS.wait(TE)
        cS.wait(V)
        aw_arena_cT = [cT.n, cT.n, cT.n, cT.n]
        proj_evac_cS = [cS.n, cS.n]

        def aw_load(w_hbm, k):
            bs = k % 4
            G.wait_ge(cT.h, aw_arena_cT[bs])
            src = bass.AP(w_hbm, (k % 4) * 2 * 128 * D,
                          [[D, 128], [128 * D, 2], [1, D]])
            dAW[bs].inc(G.dma_start(
                out=awb[:, bs * 2048:(bs + 1) * 2048], in_=src), 16)

        def aw_tile(k, et):
            bs = (k // 2) % 4
            off = bs * 2048 + (k % 2) * 1024 + et * 128
            return awb[:, off:off + 128]

        def proj(w_hbm, out_fm, out_off, func, bias_ap, scale, rhs_off, th):
            aw_load(w_hbm, 0)
            aw_load(w_hbm, 1)
            for dc in range(DC):
                kp = dc // 2
                bsel = kp % 4
                dAW[bsel].wait(TE)
                if dc == 0:
                    TE.wait_ge(cS.h, proj_evac_cS[0])
                    TE.wait_ge(cS.h, proj_evac_cS[1])
                tl_ = None
                for et in range(DC):
                    tl_ = TE.matmul(
                        out=pb[et][:, 0:HT],
                        lhsT=aw_tile(dc, et),
                        rhs=xn2T[:, dc, rhs_off + th * HT:
                                 rhs_off + th * HT + HT],
                        start=(dc == 0), stop=(dc == DC - 1))
                cT.inc(tl_)
                if dc % 2 == 1:
                    aw_arena_cT[bsel] = cT.n
                    if kp + 2 < 4:
                        aw_load(w_hbm, kp + 2)
            for eh in range(2):
                cT.wait(S)
                sl2 = None
                for ei in range(4):
                    et = eh * 4 + ei
                    sl2 = S.activation(
                        out=out_fm[:, et, out_off + th * HT:
                                   out_off + th * HT + HT],
                        in_=pb[eh * 4 + ei][:, 0:HT], func=func,
                        bias=bias_ap[:, et:et + 1], scale=scale)
                cS.inc(sl2)
                proj_evac_cS[eh] = cS.n

        def emit_half_mms(th):
            tt0 = th * 2
            transpose_quads(
                [([xstage(tt)[:, dc, :] for dc in range(DC)],
                  xn2T[:, :, 1 + tt * 128:1 + (tt + 1) * 128])
                 for tt in (tt0, tt0 + 1)])
            transpose_quads(
                [([xb[:, tt, dc * 128:(dc + 1) * 128] for dc in range(DC)],
                  VT[:, :, tt * 128:(tt + 1) * 128])
                 for tt in (tt0, tt0 + 1)])
            if th == 0:
                cV.wait(V)
                V.tensor_copy(out=xn2T[:, :, 0], in_=xnc)
                V.tensor_copy(out=QB[:, :, 0], in_=qc)
                cV.inc(V.tensor_copy(out=OAR[:, :, 0], in_=oarc))
            cV.wait(TE)
            cS.wait(TE)
            proj(wk1t, KB, 0, AF.Exp, b_k1, 1.0, 1, th)
            proj(wq1t, QB, 1, AF.Sigmoid, b_q1, 1.0, 1, th)
            proj(wk2t, K2B, 0, AF.Sigmoid, b_k2s, K2SCALE, 0, th)
            half_cS[th] = cS.n
            half_cV[th] = cV.n

        def chain(th):
            o0 = th * HT
            cV_pre = cV.n
            lane_done = [None] * 4
            GL = (2, 3)

            def lslices(l):
                dl = slice(2 * l, 2 * l + 2)
                return (dl, KB[:, dl, o0:o0 + HT], KV[:, dl, o0:o0 + HT],
                        VT[:, dl, o0:o0 + HT], K2B[:, dl, o0:o0 + HT])

            G.wait_ge(cS.h, half_cS[th])
            G.wait_ge(cV.h, cV_pre)
            V.wait_ge(cS.h, half_cS[th])
            gkv = {}
            for l in GL:
                dl, KBl, KVl, VTl, K2l = lslices(l)
                cG.wait(G)
                cG.inc(G.tensor_tensor(out=KVl, in0=KBl, in1=VTl,
                                       op=ALU.mult))
                gkv[l] = cG.n
            kc_done, kvc_done, rkc_done = {}, {}, {}
            for l in GL:
                dl = slice(2 * l, 2 * l + 2)
                V.wait_ge(cG.h, gkv[l])
                cV.wait(V)
                for dc in (2 * l, 2 * l + 1):
                    ini = (kc_carry[:, dc:dc + 1] if th == 0
                           else kc_mid[:, dc:dc + 1])
                    ll = V.tensor_tensor_scan(out=KB[:, dc, o0:o0 + HT],
                                              data0=KB[:, dc, o0:o0 + HT],
                                              data1=KB[:, dc, o0:o0 + HT],
                                              initial=ini,
                                              op0=ALU.add, op1=ALU.bypass)
                cV.inc(ll)
                kc_done[l] = cV.n
            for l in GL:
                dl = slice(2 * l, 2 * l + 2)
                cV.wait(V)
                for dc in (2 * l, 2 * l + 1):
                    ini = (kv_carry[:, dc:dc + 1] if th == 0
                           else kv_mid[:, dc:dc + 1])
                    ll = V.tensor_tensor_scan(out=KV[:, dc, o0:o0 + HT],
                                              data0=KV[:, dc, o0:o0 + HT],
                                              data1=KV[:, dc, o0:o0 + HT],
                                              initial=ini,
                                              op0=ALU.add, op1=ALU.bypass)
                cV.inc(ll)
                cV.wait(V)
                mdst = (kc_mid, kv_mid) if th == 0 else (kc_carry, kv_carry)
                V.tensor_copy(out=mdst[0][:, dl], in_=KB[:, dl, o0 + HT - 1])
                cV.inc(V.tensor_copy(out=mdst[1][:, dl],
                                     in_=KV[:, dl, o0 + HT - 1]))
                kvc_done[l] = cV.n
            for l in GL:
                dl, KBl, KVl, VTl, K2l = lslices(l)
                S.wait_ge(cV.h, kvc_done[l])
                cS.wait(S)
                cS.inc(S.activation(out=KBl, in_=KBl, func=AF.Ln,
                                    bias=epskc_ap))
                cS.wait(S)
                cS.inc(S.activation(out=KBl, in_=KBl, func=AF.Exp,
                                    scale=-1.0))
                rkc_done[l] = cS.n

            for l in (0, 1):
                dl, KBl, KVl, VTl, K2l = lslices(l)
                cV.wait(V)
                cV.inc(V.tensor_tensor(out=KVl, in0=KBl, in1=VTl,
                                       op=ALU.mult))
                cV.wait(V)
                for dc in (2 * l, 2 * l + 1):
                    ini = (kc_carry[:, dc:dc + 1] if th == 0
                           else kc_mid[:, dc:dc + 1])
                    ll = V.tensor_tensor_scan(out=KB[:, dc, o0:o0 + HT],
                                              data0=KB[:, dc, o0:o0 + HT],
                                              data1=KB[:, dc, o0:o0 + HT],
                                              initial=ini,
                                              op0=ALU.add, op1=ALU.bypass)
                cV.inc(ll)
                cV.wait(V)
                for dc in (2 * l, 2 * l + 1):
                    ini = (kv_carry[:, dc:dc + 1] if th == 0
                           else kv_mid[:, dc:dc + 1])
                    ll = V.tensor_tensor_scan(out=KV[:, dc, o0:o0 + HT],
                                              data0=KV[:, dc, o0:o0 + HT],
                                              data1=KV[:, dc, o0:o0 + HT],
                                              initial=ini,
                                              op0=ALU.add, op1=ALU.bypass)
                cV.inc(ll)
                cV.wait(V)
                mdst = (kc_mid, kv_mid) if th == 0 else (kc_carry, kv_carry)
                V.tensor_copy(out=mdst[0][:, dl], in_=KB[:, dl, o0 + HT - 1])
                cV.inc(V.tensor_copy(out=mdst[1][:, dl],
                                     in_=KV[:, dl, o0 + HT - 1]))
                cV.wait(S)
                cS.inc(S.activation(out=KBl, in_=KBl, func=AF.Ln,
                                    bias=epskc_ap))
                cS.wait(S)
                cS.inc(S.activation(out=KBl, in_=KBl, func=AF.Exp,
                                    scale=-1.0))
                cS.wait(V)
                cV.wait(V)
                cV.inc(V.tensor_tensor(out=KVl, in0=KVl, in1=KBl,
                                       op=ALU.mult))
                cV.wait(V)
                cV.inc(V.tensor_tensor(out=OAR[:, dl, o0 + 1:o0 + HT + 1],
                                       in0=KVl,
                                       in1=QB[:, dl, o0 + 1:o0 + HT + 1],
                                       op=ALU.mult))
                cV.wait(V)
                cV.inc(V.tensor_tensor(out=KBl, in0=VTl,
                                       in1=OAR[:, dl, o0:o0 + HT],
                                       op=ALU.subtract))
                if c == 0 and th == 0:
                    cV.wait(V)
                    cV.inc(V.memset(KB[:, dl, 0:1], 0.0))
                cV.wait(V)
                cV.inc(V.tensor_tensor(out=KVl, in0=K2l, in1=KBl,
                                       op=ALU.mult))
                cV.wait(V)
                for dc in (2 * l, 2 * l + 1):
                    ini = (s2_carry[:, dc:dc + 1] if th == 0
                           else s2_mid[:, dc:dc + 1])
                    ll = V.tensor_tensor_scan(out=KV[:, dc, o0:o0 + HT],
                                              data0=KV[:, dc, o0:o0 + HT],
                                              data1=KV[:, dc, o0:o0 + HT],
                                              initial=ini,
                                              op0=ALU.add, op1=ALU.bypass)
                cV.inc(ll)
                cV.wait(V)
                mdst = s2_mid if th == 0 else s2_carry
                cV.inc(V.tensor_copy(out=mdst[:, dl],
                                     in_=KV[:, dl, o0 + HT - 1]))
                cV.wait(V)
                cV.inc(V.scalar_tensor_tensor(out=VTl, in0=KVl,
                                              scalar=K2SCALE,
                                              in1=QB[:, dl, o0:o0 + HT],
                                              op0=ALU.mult, op1=ALU.mult))
                cV.wait(V)
                cV.inc(V.tensor_tensor(out=OB_[:, dl, o0:o0 + HT], in0=VTl,
                                       in1=OAR[:, dl, o0 + 1:o0 + HT + 1],
                                       op=ALU.add))
                lane_done[l] = (cV, cV.n)
            k2e_done = {}
            for l in GL:
                dl, KBl, KVl, VTl, K2l = lslices(l)
                G.wait_ge(cV.h, kvc_done[l])
                G.wait_ge(cS.h, rkc_done[l])
                cG.wait(G)
                cG.inc(G.tensor_tensor(out=KVl, in0=KVl, in1=KBl,
                                       op=ALU.mult))
                cG.wait(G)
                cG.inc(G.tensor_tensor(out=OAR[:, dl, o0 + 1:o0 + HT + 1],
                                       in0=KVl,
                                       in1=QB[:, dl, o0 + 1:o0 + HT + 1],
                                       op=ALU.mult))
                cG.wait(G)
                cG.inc(G.tensor_tensor(out=KBl, in0=VTl,
                                       in1=OAR[:, dl, o0:o0 + HT],
                                       op=ALU.subtract))
                if c == 0 and th == 0:
                    cG.wait(G)
                    cG.inc(G.memset(KB[:, dl, 0:1], 0.0))
                cG.wait(G)
                cG.inc(G.tensor_tensor(out=KVl, in0=K2l, in1=KBl,
                                       op=ALU.mult))
                k2e_done[l] = cG.n

            oma_done = {}
            for l in GL:
                dl, KBl, KVl, VTl, K2l = lslices(l)
                V.wait_ge(cG.h, k2e_done[l])
                cV.wait(V)
                for dc in (2 * l, 2 * l + 1):
                    ini = (s2_carry[:, dc:dc + 1] if th == 0
                           else s2_mid[:, dc:dc + 1])
                    ll = V.tensor_tensor_scan(out=KV[:, dc, o0:o0 + HT],
                                              data0=KV[:, dc, o0:o0 + HT],
                                              data1=KV[:, dc, o0:o0 + HT],
                                              initial=ini,
                                              op0=ALU.add, op1=ALU.bypass)
                cV.inc(ll)
                cV.wait(V)
                mdst = s2_mid if th == 0 else s2_carry
                cV.inc(V.tensor_copy(out=mdst[:, dl],
                                     in_=KV[:, dl, o0 + HT - 1]))
                cV.wait(V)
                cV.inc(V.scalar_tensor_tensor(out=VTl, in0=KVl,
                                              scalar=K2SCALE,
                                              in1=QB[:, dl, o0:o0 + HT],
                                              op0=ALU.mult, op1=ALU.mult))
                oma_done[l] = cV.n
            for l in GL:
                dl = slice(2 * l, 2 * l + 2)
                G.wait_ge(cV.h, oma_done[l])
                cG.wait(G)
                cG.inc(G.tensor_tensor(out=OB_[:, dl, o0:o0 + HT],
                                       in0=VT[:, dl, o0:o0 + HT],
                                       in1=OAR[:, dl, o0 + 1:o0 + HT + 1],
                                       op=ALU.add))
                lane_done[l] = (cG, cG.n)
            if th == 1:
                cV.wait(V)
                cG.wait(V)
                V.tensor_copy(out=oarc, in_=OAR[:, :, 512])
                V.tensor_copy(out=qc, in_=QB[:, :, 512])
                cV.inc(V.tensor_copy(out=xnc, in_=xn2T[:, :, 512]))
            return lane_done

        def wproj_mms(th, lane_done):
            TE.wait_ge(cV.h, wp_evac_cV[th])
            TE.wait_ge(cS.h, proj_evac_cS[th])
            TE.wait_ge(lane_done[0][0].h, lane_done[0][1])
            for tt in (th * 2, th * 2 + 1):
                for eb in range(2):
                    TE.matmul(out=pb[th * 4 + (tt % 2) * 2 + eb][:, :],
                              lhsT=onesrb[:, :],
                              rhs=bprow[:, eb * 512:(eb + 1) * 512],
                              start=True, stop=False)
            for dc in range(DC):
                kp = dc // 2
                bsel = kp % 4
                dAW[bsel].wait(TE)
                if dc % 2 == 0:
                    TE.wait_ge(lane_done[kp][0].h, lane_done[kp][1])
                tl_ = None
                for tt in (th * 2, th * 2 + 1):
                    for eb in range(2):
                        tl_ = TE.matmul(
                            out=pb[th * 4 + (tt % 2) * 2 + eb][:, :],
                            lhsT=OB_[:, dc, tt * 128:(tt + 1) * 128],
                            rhs=aw_tile(dc, 0)[:, 0:0 + 512] if False else
                            awb[:, bsel * 2048 + (dc % 2) * 1024 + eb * 512:
                                bsel * 2048 + (dc % 2) * 1024 + (eb + 1) * 512],
                            start=False, stop=(dc == DC - 1))
                cT.inc(tl_)
                if dc % 2 == 1:
                    aw_arena_cT[bsel] = cT.n
                    if kp + 2 < 4:
                        aw_load(wpt, kp + 2)

        def wproj_evac(th):
            cT.wait(V)
            vl_ = None
            for tt in (th * 2, th * 2 + 1):
                for eb in range(2):
                    vl_ = V.tensor_tensor(
                        out=xb[:, tt, eb * 512:(eb + 1) * 512],
                        in0=pb[th * 4 + (tt % 2) * 2 + eb][:, :],
                        in1=xb[:, tt, eb * 512:(eb + 1) * 512],
                        op=ALU.add)
            cV.inc(vl_)
            wp_evac_cV[th] = cV.n

        wp_evac_cV = [cV.n, cV.n]
        half_cS = [cS.n, cS.n]
        half_cV = [cV.n, cV.n]
        if c == 0:
            for grp in (tern_done["gu"], tern_done["d"]):
                for s_, n_ in grp:
                    s_.wait(G, n_)
        emit_half_mms(0)
        attn_pre_tts(xb, (2, 3))
        emit_half_mms(1)
        if DEBUG_ATTN and c == 0:
            cS.wait(SY)
            cV.wait(SY)
            dDB.inc(SY.dma_start(out=bass.AP(dbg2, 0, [[4104, 128], [1, 4104]]),
                                 in_=pool[:, 8192:12296]), 16)
            dDB.wait(V)
            dDB.wait(S)
            dDB.wait(TE)
        if c == 0:
            for s_, n_ in tern_done["ti"] + tern_done["gu"] + tern_done["d"]:
                s_.wait(S, n_)
                s_.wait(V, n_)
        dBP.wait(TE, 16)
        aw_load(wpt, 0)
        aw_load(wpt, 1)
        lanes0 = chain(0)
        wproj_mms(0, lanes0)
        aw_load(wpt, 0)
        aw_load(wpt, 1)
        lanes1 = chain(1)
        if DEBUG_ATTN and c == 0:
            cV.wait(SY)
            cS.wait(SY)
            for i_, ap_ in enumerate([pool[:, 4096:8192], pool[:, 12296:16392],
                                      pool[:, 0:4096], pool[:, 16392:20496],
                                      pool[:, 8192:12296], pool[:, 20496:24592]]):
                n_ = ap_.shape[1]
                dDB.inc(SY.dma_start(out=bass.AP(dbg, i_ * 128 * 4104,
                                                [[4104, 128], [1, n_]]),
                                    in_=ap_), 16)
            dDB.inc(SY.dma_start(out=bass.AP(dbgq, 0, [[4104, 128], [1, 4104]]),
                                in_=poolb[:, 4096:8200]), 16)
            dDB.inc(SY.dma_start(out=bass.AP(dbgq, 128 * 4104,
                                            [[4104, 128], [1, 4096]]),
                                in_=poolb[:, 8200:12296]), 16)
            dDB.wait(V)
            dDB.wait(S)
            dDB.wait(TE)
        wproj_evac(0)
        ln_stats_part(xb, (0, 1))
        wproj_mms(1, lanes1)
        wproj_evac(1)

    emit_ffn_pre(0, xsb[0], True, 0)
    emit_tern_stream()
    tern_done["gu"] = [(x, x.n) for x in dTG + dTU]
    tern_done["d"] = [(x, x.n) for x in dTD]
    tern_done["ti"] = [(x, x.n) for x in dTI]

    for c in range(NCH):
        xb = xsb[c % 2]
        if c + 1 < NCH:
            if c >= 1:
                dO.wait(SY, 16 * c)
            dX[(c + 1) % 2].inc(SY.dma_start(
                out=xsb[(c + 1) % 2][:, :, :],
                in_=x_hbm.ap()[(c + 1) * T:(c + 2) * T, :]
                .rearrange("(a p) d -> p a d", p=128)), 16)
        emit_ffn_rest(c, xb, c % 2,
                      attn_next=(lambda: attn_pre_tts(xb, (0, 1))))
        emit_attn(c, xb, pre01_done=True)
        emit_ffn_pre(c, xb, False, c % 2, skip01=True)
        pn = None
        if c + 1 < NCH:
            pn = (lambda cc: (lambda: emit_ffn_pre(cc, xsb[cc % 2], True,
                                                   cc % 2)))(c + 1)
        emit_ffn_rest(c, xb, c % 2, pre_next=pn)
        cV.wait(SY)
        cS.wait(SY)
        dO.inc(SY.dma_start(
            out=y_hbm.ap()[c * T:(c + 1) * T, :].rearrange("(a p) d -> p a d",
                                                           p=128),
            in_=xb[:, :, :]), 16)
    SY.wait_ge(dO.h, 16 * NCH)
    st.close()
    return nc


def kernel(**inp):
    def f32c(a):
        return np.ascontiguousarray(np.asarray(a, dtype=np.float32))

    x = f32c(inp["L0_attn_x"])
    B, L, _ = x.shape
    wT = {
        "wq1t": f32c(np.asarray(inp["L0_attn_Wq1"]).T),
        "wk1t": f32c(np.asarray(inp["L0_attn_Wk1"]).T),
        "wk2t": f32c(np.asarray(inp["L0_attn_Wk2"]).T),
        "wpt": f32c(np.asarray(inp["L0_attn_Wproj"]).T),
        "wgt": f32c(np.asarray(inp["L0_ffn_w_gate"]).T),
        "wut": f32c(np.asarray(inp["L0_ffn_w_up"]).T),
        "wdt": f32c(np.asarray(inp["L0_ffn_w_down"]).T),
        "bq1": f32c(inp["L0_attn_bq1"]),
        "bk1": f32c(inp["L0_attn_bk1"]),
        "bk2": f32c(inp["L0_attn_bk2"]),
        "bpr": f32c(inp["L0_attn_bproj"]),
    }
    nc = build_nc(L=L, T=512)
    in_maps = [dict(x=x[c], **wT) for c in range(B)]
    res = run_bass_kernel_spmd(nc, in_maps, core_ids=list(range(B)))
    return np.stack([np.asarray(res.results[c]["y"], dtype=np.float32)
                     for c in range(B)], axis=0)
